# version 02
# baseline (speedup 1.0000x reference)
import os
from contextlib import ExitStack

import numpy as np
import ml_dtypes
import concourse.bass as bass
import concourse.mybir as mybir
from concourse.bass_utils import run_bass_kernel_spmd

F32 = mybir.dt.float32
BF16 = mybir.dt.bfloat16
AF = mybir.ActivationFunctionType
ALU = mybir.AluOpType
AX = mybir.AxisListType

D = 2048
HID = 8192
NH = 16
NCORES = 8
EPS = 1e-5
RSQRT_N = float(1.0 / np.sqrt(HID))

ALL_PHASES = ("a1a", "a1b", "a2", "a3a", "a3b", "a3c", "bprep", "b")


class Sem:
    def __init__(self, nc, name):
        self.h = nc.alloc_semaphore(name=name)
        self.n = 0

    def inc(self, instr, k=1):
        instr.then_inc(self.h, k)
        self.n += k
        return self.n


def build_program(T=2048, phases=ALL_PHASES, debug_dump=()):
    assert T % 1024 == 0
    TT = T // 128
    NCH = T // 128
    NCHB = NCH // 2
    NTC = T // 512
    TB = T // 2

    nc = bass.Bass("TRN2", target_bir_lowering=False, debug=False,
                   detect_race_conditions=False)

    x_in = nc.dram_tensor("x", [T, D], F32, kind="ExternalInput").ap()
    wq_in = nc.dram_tensor("wq", [D, D], F32, kind="ExternalInput").ap()
    wk_in = nc.dram_tensor("wk", [D, D], F32, kind="ExternalInput").ap()
    wv_in = nc.dram_tensor("wv", [D, D], F32, kind="ExternalInput").ap()
    wb_in = nc.dram_tensor("wbeta", [D, D], F32, kind="ExternalInput").ap()
    wo_in = nc.dram_tensor("wo", [D, D], F32, kind="ExternalInput").ap()
    w1_in = nc.dram_tensor("w1", [HID, D], F32, kind="ExternalInput").ap()
    w2_in = nc.dram_tensor("w2", [D, HID], F32, kind="ExternalInput").ap()
    w3_in = nc.dram_tensor("w3", [HID, D], F32, kind="ExternalInput").ap()
    h128_in = nc.dram_tensor("h128", [128, 128], BF16, kind="ExternalInput").ap()
    h64_in = nc.dram_tensor("h64", [64, 64], BF16, kind="ExternalInput").ap()
    maskT_in = nc.dram_tensor("maskT", [128, 128], BF16, kind="ExternalInput").ap()
    ident_in = nc.dram_tensor("ident", [128, 128], F32, kind="ExternalInput").ap()
    pmask_in = nc.dram_tensor("pmask", [128, 8], F32, kind="ExternalInput").ap()
    out = nc.dram_tensor("out", [T, D], F32, kind="ExternalOutput").ap()

    xn_d = nc.dram_tensor("xn_d", [T, D], BF16, kind="Internal").ap()
    k_d = nc.dram_tensor("k_d", [T, D], BF16, kind="Internal").ap()
    u_d = nc.dram_tensor("u_d", [T, D], BF16, kind="Internal").ap()
    q_d = nc.dram_tensor("q_d", [D, T], BF16, kind="Internal").ap()
    s_loc = nc.dram_tensor("s_loc", [32 * 128, 128], F32, kind="Internal")
    s_all = nc.dram_tensor("s_all", [NCORES * 32 * 128, 128], F32, kind="Internal")
    x2_d = nc.dram_tensor("x2_d", [T, D], F32, kind="Internal").ap()
    xn2_d = nc.dram_tensor("xn2_d", [T, D], BF16, kind="Internal").ap()
    w1b = nc.dram_tensor("w1b", [HID, D], BF16, kind="Internal").ap()
    w3b = nc.dram_tensor("w3b", [HID, D], BF16, kind="Internal").ap()
    w2hb = nc.dram_tensor("w2hb", [D, HID], BF16, kind="Internal").ap()
    w1tb = nc.dram_tensor("w1tb", [D, HID], BF16, kind="Internal").ap()
    w3tb = nc.dram_tensor("w3tb", [D, HID], BF16, kind="Internal").ap()
    w2htb = nc.dram_tensor("w2htb", [HID, D], BF16, kind="Internal").ap()

    o_dbg = nc.dram_tensor("o_dbg_d", [128, 16 * T], BF16, kind="Internal").ap()
    sp_dbg = nc.dram_tensor("sp_dbg_d", [128, 16 * TB + 2048 + 6144 + 2048], BF16, kind="Internal").ap()
    dbg_specs = {
        "scanprobe": (sp_dbg, [128, 16 * TB + 2048 + 6144 + 2048], BF16),
        "o_fm": (o_dbg, [128, 16 * T], BF16),
        "xn_d": (xn_d, [T, D], BF16), "k_d": (k_d, [T, D], BF16),
        "u_d": (u_d, [T, D], BF16), "q_d": (q_d, [D, T], BF16),
        "s_loc": (s_loc.ap(), [32 * 128, 128], F32),
        "s_all": (s_all.ap(), [NCORES * 32 * 128, 128], F32),
        "x2_d": (x2_d, [T, D], F32), "xn2_d": (xn2_d, [T, D], BF16),
        "w1b": (w1b, [HID, D], BF16), "w2hb": (w2hb, [D, HID], BF16),
        "w1tb": (w1tb, [D, HID], BF16), "w3tb": (w3tb, [D, HID], BF16),
        "w2htb": (w2htb, [HID, D], BF16),
    }
    dbg = {n: nc.dram_tensor("dbg_" + n, dbg_specs[n][1], dbg_specs[n][2],
                             kind="ExternalOutput").ap() for n in debug_dump}

    k_dv = k_d.rearrange("(r p) c -> p r c", p=128)
    u_dv = u_d.rearrange("(r p) c -> p r c", p=128)
    q_dv = q_d.rearrange("(r p) c -> p r c", p=128)

    sems = {}
    done = {}

    with (
        nc.sbuf_tensor("h128_s", [128, 128], BF16) as h128_s,
        nc.sbuf_tensor("h64_s", [64, 64], BF16) as h64_s,
        nc.sbuf_tensor("maskT_s", [128, 128], BF16) as maskT_s,
        nc.sbuf_tensor("ident_s", [128, 128], F32) as ident_s,
        nc.sbuf_tensor("pmask_s", [128, 8], F32) as pmask_s,
        nc.sbuf_tensor("stats", [128, 128], F32) as stats,
        nc.sbuf_tensor("xio", [128, 2 * D], F32) as xio,
        nc.sbuf_tensor("sq_s", [128, 2 * D], F32) as sq_s,
        nc.sbuf_tensor("xnb", [128, 2 * D], BF16) as xnb,
        nc.sbuf_tensor("bufA", [128, 16 * 2048], BF16) as bufA,
        nc.psum_tensor("ps0", [128, 512], F32) as ps0,
        nc.psum_tensor("ps1", [128, 512], F32) as ps1,
        nc.psum_tensor("ps2", [128, 512], F32) as ps2,
        nc.psum_tensor("ps3", [128, 512], F32) as ps3,
        nc.psum_tensor("ps4", [128, 512], F32) as ps4,
        nc.psum_tensor("ps5", [128, 512], F32) as ps5,
        nc.psum_tensor("ps6", [128, 512], F32) as ps6,
        nc.psum_tensor("ps7", [128, 512], F32) as ps7,
        nc.Block() as block,
    ):
        PS = [ps0, ps1, ps2, ps3, ps4, ps5, ps6, ps7]

        def sem(name):
            if name not in sems:
                sems[name] = Sem(nc, name)
            return sems[name]

        s_const = sem("const")

        @block.sync
        def _(sy):
            for dst, src in ((h128_s, h128_in), (h64_s, h64_in), (maskT_s, maskT_in),
                             (ident_s, ident_in), (pmask_s, pmask_in)):
                sy.dma_start(out=dst[:, :], in_=src).then_inc(s_const.h, 16)
        s_const.n = 80
        done["const"] = [(s_const, 80)]

        def ln_batched(tag, src_dram, dst_dram, pre_wait):
            s_ld = sem(tag + "_ld")
            s_r1 = sem(tag + "_r1")
            s_sq = sem(tag + "_sq")
            s_r2 = sem(tag + "_r2")
            s_ch = sem(tag + "_ch")
            s_ld2 = sem(tag + "_ld2")
            s_ap = sem(tag + "_ap")
            s_st = sem(tag + "_st")

            @block.sync
            def _(sy):
                for sm, cn in pre_wait:
                    sy.wait_ge(sm.h, cn)
                for tt in range(TT):
                    if tt >= 2:
                        sy.wait_ge(s_r1.h, tt - 1)
                        sy.wait_ge(s_sq.h, tt - 1)
                    sy.dma_start(out=xio[:, (tt % 2) * D:(tt % 2 + 1) * D],
                                 in_=src_dram[tt * 128:(tt + 1) * 128, :]).then_inc(s_ld.h, 16)
                    s_ld.n += 16
                for tt in range(TT):
                    if tt < 2:
                        sy.wait_ge(s_ch.h, 1)
                    else:
                        sy.wait_ge(s_ap.h, tt - 1)
                    sy.dma_start(out=xio[:, (tt % 2) * D:(tt % 2 + 1) * D],
                                 in_=src_dram[tt * 128:(tt + 1) * 128, :]).then_inc(s_ld2.h, 16)
                    s_ld2.n += 16

            @block.scalar
            def _(sc):
                for tt in range(TT):
                    sc.wait_ge(s_ld.h, (tt + 1) * 16)
                    if tt >= 2:
                        sc.wait_ge(s_r2.h, tt - 1)
                    sc.activation(sq_s[:, (tt % 2) * D:(tt % 2 + 1) * D],
                                  xio[:, (tt % 2) * D:(tt % 2 + 1) * D],
                                  AF.Square).then_inc(s_sq.h, 1)
                    s_sq.n += 1
                sc.wait_ge(s_ch.h, 1)
                sc.activation(stats[:, 48:48 + TT], stats[:, 80:80 + TT], AF.Ln)
                sc.drain()
                sc.activation(stats[:, 96:96 + TT], stats[:, 48:48 + TT], AF.Exp,
                              scale=-0.5).then_inc(s_ch.h, 1)
                s_ch.n += 1
                for tt in range(TT):
                    sc.wait_ge(s_ch.h, 3)
                    sc.wait_ge(s_ld2.h, (tt + 1) * 16)
                    if tt >= 2:
                        sc.wait_ge(s_st.h, (tt - 1) * 16)
                    sc.activation(xnb[:, (tt % 2) * D:(tt % 2 + 1) * D],
                                  xio[:, (tt % 2) * D:(tt % 2 + 1) * D], AF.Identity,
                                  bias=stats[:, 112 + tt:113 + tt],
                                  scale=stats[:, 96 + tt:97 + tt]).then_inc(s_ap.h, 1)
                    s_ap.n += 1

            @block.vector
            def _(ve):
                for tt in range(TT):
                    ve.wait_ge(s_ld.h, (tt + 1) * 16)
                    ve.reduce_sum(stats[:, tt:tt + 1],
                                  xio[:, (tt % 2) * D:(tt % 2 + 1) * D],
                                  axis=AX.X).then_inc(s_r1.h, 1)
                    s_r1.n += 1
                    ve.wait_ge(s_sq.h, tt + 1)
                    ve.reduce_sum(stats[:, 16 + tt:17 + tt],
                                  sq_s[:, (tt % 2) * D:(tt % 2 + 1) * D],
                                  axis=AX.X).then_inc(s_r2.h, 1)
                    s_r2.n += 1
                ve.drain()
                ve.tensor_scalar(stats[:, 32:32 + TT], stats[:, 0:TT], 1.0 / D, None, ALU.mult)
                ve.tensor_scalar(stats[:, 80:80 + TT], stats[:, 16:16 + TT], 1.0 / D, None, ALU.mult)
                ve.drain()
                ve.tensor_tensor(stats[:, 64:64 + TT], stats[:, 32:32 + TT],
                                 stats[:, 32:32 + TT], ALU.mult)
                ve.drain()
                ve.tensor_tensor(stats[:, 80:80 + TT], stats[:, 80:80 + TT],
                                 stats[:, 64:64 + TT], ALU.subtract)
                ve.drain()
                ve.tensor_scalar(stats[:, 80:80 + TT], stats[:, 80:80 + TT],
                                 EPS, None, ALU.add).then_inc(s_ch.h, 1)
                s_ch.n += 1
                ve.wait_ge(s_ch.h, 2)
                ve.scalar_tensor_tensor(stats[:, 112:112 + TT], stats[:, 32:32 + TT],
                                        -1.0, stats[:, 96:96 + TT],
                                        ALU.mult, ALU.mult).then_inc(s_ch.h, 1)
                s_ch.n += 1

            @block.gpsimd
            def _(g):
                for tt in range(TT):
                    g.wait_ge(s_ap.h, tt + 1)
                    g.dma_start(out=dst_dram[tt * 128:(tt + 1) * 128, :],
                                in_=xnb[:, (tt % 2) * D:(tt % 2 + 1) * D]).then_inc(s_st.h, 16)
                    s_st.n += 16
            return [(s_st, TT * 16)]

        def ln_tile_vec(ve, sv, sa, i, src, extra_op):
            ve.reduce_sum(stats[:, 0:1], src, axis=AX.X)
            ve.drain()
            ve.tensor_scalar(stats[:, 2:3], stats[:, 0:1], 1.0 / D, None,
                             ALU.mult).then_inc(sv.h, 1)
            sv.n += 1
            ve.wait_ge(sa.h, 3 * i + 1)
            ve.reduce_sum(stats[:, 1:2], sq_s[:, 0:D], axis=AX.X)
            ve.drain()
            ve.tensor_scalar(stats[:, 3:4], stats[:, 1:2], 1.0 / D, None, ALU.mult)
            ve.drain()
            ve.tensor_tensor(stats[:, 4:5], stats[:, 2:3], stats[:, 2:3], ALU.mult)
            ve.drain()
            ve.tensor_tensor(stats[:, 5:6], stats[:, 3:4], stats[:, 4:5], ALU.subtract)
            ve.drain()
            ve.tensor_scalar(stats[:, 5:6], stats[:, 5:6], EPS, None,
                             ALU.add).then_inc(sv.h, 1)
            sv.n += 1
            ve.wait_ge(sa.h, 3 * i + 2)
            ve.scalar_tensor_tensor(stats[:, 7:8], stats[:, 2:3], -1.0,
                                    stats[:, 6:7], ALU.mult, ALU.mult).then_inc(sv.h, 1)
            sv.n += 1
            ve.wait_ge(sa.h, 3 * i + 3)
            extra_op(ve)
            sv.n += 1

        def ln_tile_act(sc, sv, sa, i, src, apply_dst):
            sc.activation(sq_s[:, 0:D], src, AF.Square).then_inc(sa.h, 1)
            sa.n += 1
            sc.wait_ge(sv.h, 4 * i + 2)
            sc.activation(stats[:, 6:7], stats[:, 5:6], AF.Ln)
            sc.drain()
            sc.activation(stats[:, 6:7], stats[:, 6:7], AF.Exp,
                          scale=-0.5).then_inc(sa.h, 1)
            sa.n += 1
            sc.wait_ge(sv.h, 4 * i + 3)
            sc.activation(apply_dst, src, AF.Identity, bias=stats[:, 7:8],
                          scale=stats[:, 6:7]).then_inc(sa.h, 1)
            sa.n += 1

        if "a1a" in phases:
            done["a1a"] = ln_batched("ln1", x_in, xn_d, [])

        if "a1b" in phases:
            if True:
                st_a1b = ExitStack()
                s_sb = st_a1b.enter_context(nc.sbuf_tensor("s_sb", [128, 32 * 128], F32))
                wsl = st_a1b.enter_context(nc.sbuf_tensor("wsl", [128, 2 * 16 * 512], BF16))
                kvu = st_a1b.enter_context(nc.sbuf_tensor("kvu", [128, 3 * TT * 512], BF16))
                sig2 = st_a1b.enter_context(nc.sbuf_tensor("sig2", [128, 2 * 512], F32))
                VOFF, KOFF, UOFF = 0, TT * 512, 2 * TT * 512
                s_xnfm = sem("a1b_xnfm")
                s_wld = sem("a1b_wld")
                s_pmm = sem("a1b_pmm")
                s_pev = sem("a1b_pev")
                s_uev = sem("a1b_uev")
                s_smm = sem("a1b_smm")
                s_sev = sem("a1b_sev")
                s_spill = sem("a1b_spill")

                @block.sync
                def _(sy):
                    sy.wait_ge(sems["ln1_st"].h, TT * 16)
                    for dsl in range(16):
                        sy.dma_start(out=bufA[:, dsl * T:(dsl + 1) * T],
                                     in_=xn_d[:, dsl * 128:(dsl + 1) * 128],
                                     transpose=True).then_inc(s_xnfm.h, 16)
                    s_xnfm.n = 256

                W3L = [wv_in, wb_in, wk_in]

                def emit_spill(g, ec):
                    g.wait_ge(s_uev.h, (ec + 1) * TT)
                    g.dma_start(out=u_dv[:, :, ec * 512:(ec + 1) * 512],
                                in_=kvu[:, UOFF:UOFF + TT * 512].rearrange(
                                    "p (ts e) -> p ts e", e=512)).then_inc(s_spill.h, 16)
                    g.wait_ge(s_pev.h, (ec * 3 + 2) * TT + TT)
                    g.dma_start(out=k_dv[:, :, ec * 512:(ec + 1) * 512],
                                in_=kvu[:, KOFF:KOFF + TT * 512].rearrange(
                                    "p (ts e) -> p ts e", e=512)).then_inc(s_spill.h, 16)
                    s_spill.n += 32

                @block.gpsimd
                def _(g):
                    g.wait_ge(s_xnfm.h, 256)
                    idx = 0
                    for ec in range(4):
                        if ec > 0:
                            emit_spill(g, ec - 1)
                        for wi in range(3):
                            if idx >= 2:
                                g.wait_ge(s_pmm.h, (idx - 1) * TT)
                            src = W3L[wi].rearrange("(dt p) e -> p dt e", p=128)[
                                :, :, ec * 512:(ec + 1) * 512]
                            g.dma_start(
                                out=wsl[:, (idx % 2) * 8192:(idx % 2 + 1) * 8192].rearrange(
                                    "p (dt e) -> p dt e", e=512),
                                in_=src).then_inc(s_wld.h, 16)
                            s_wld.n += 16
                            idx += 1
                    emit_spill(g, 3)
                    g.wait_ge(s_sev.h, 8)
                    g.dma_start(out=s_loc.ap().rearrange("(bh e) d -> e bh d", e=128),
                                in_=s_sb[:, :].rearrange("p (bh d) -> p bh d", d=128)
                                ).then_inc(s_spill.h, 16)
                    s_spill.n += 16

                @block.tensor
                def _(t):
                    idx = 0
                    for ec in range(4):
                        for wi in range(3):
                            t.wait_ge(s_wld.h, (idx + 1) * 16)
                            for ts in range(TT):
                                gi = idx * TT + ts
                                pb = PS[gi % 2]
                                if gi >= 2:
                                    t.wait_ge(s_pev.h, gi - 1)
                                for dt in range(16):
                                    mm = t.matmul(
                                        pb[:, :],
                                        bufA[:, dt * T + ts * 128: dt * T + (ts + 1) * 128],
                                        wsl[:, (idx % 2) * 8192 + dt * 512:
                                            (idx % 2) * 8192 + (dt + 1) * 512],
                                        start=(dt == 0), stop=(dt == 15))
                                mm.then_inc(s_pmm.h, 1)
                                s_pmm.n += 1
                            idx += 1
                        t.wait_ge(s_pev.h, (idx - 1) * TT + TT)
                        t.wait_ge(s_uev.h, (ec + 1) * TT)
                        if ec >= 1:
                            t.wait_ge(s_sev.h, ec * 2)
                        for hl in range(4):
                            for ch in range(NCH):
                                pb = PS[2] if ch < NCHB else PS[3]
                                mm = t.matmul(
                                    pb[:, hl * 128:(hl + 1) * 128],
                                    kvu[:, KOFF + ch * 512 + hl * 128:
                                        KOFF + ch * 512 + (hl + 1) * 128],
                                    kvu[:, UOFF + ch * 512 + hl * 128:
                                        UOFF + ch * 512 + (hl + 1) * 128],
                                    start=(ch % NCHB == 0), stop=(ch % NCHB == NCHB - 1))
                        mm.then_inc(s_smm.h, 1)
                        s_smm.n += 1

                @block.scalar
                def _(sc):
                    idx = 0
                    for ec in range(4):
                        for wi in range(3):
                            for ts in range(TT):
                                gi = idx * TT + ts
                                pb = PS[gi % 2]
                                sc.wait_ge(s_pmm.h, gi + 1)
                                if ec >= 1 and wi == 0 and ts == 0:
                                    sc.wait_ge(s_smm.h, ec)
                                    sc.wait_ge(s_spill.h, ec * 32)
                                if wi == 0:
                                    sc.activation(kvu[:, VOFF + ts * 512:VOFF + (ts + 1) * 512],
                                                  pb[:, :], AF.Copy).then_inc(s_pev.h, 1)
                                elif wi == 1:
                                    bi = ec * TT + ts
                                    if bi >= 2:
                                        sc.wait_ge(s_uev.h, bi - 1)
                                    sc.activation(sig2[:, (bi % 2) * 512:(bi % 2 + 1) * 512],
                                                  pb[:, :], AF.Sigmoid).then_inc(s_pev.h, 1)
                                else:
                                    sc.activation(kvu[:, KOFF + ts * 512:KOFF + (ts + 1) * 512],
                                                  pb[:, :], AF.Copy).then_inc(s_pev.h, 1)
                                s_pev.n += 1
                            idx += 1
                        sc.wait_ge(s_smm.h, ec + 1)
                        for b in range(2):
                            pb = PS[2 + b]
                            for hl in range(4):
                                bh = b * 16 + ec * 4 + hl
                                ins = sc.activation(s_sb[:, bh * 128:(bh + 1) * 128],
                                                    pb[:, hl * 128:(hl + 1) * 128], AF.Copy)
                            ins.then_inc(s_sev.h, 1)
                            s_sev.n += 1

                @block.vector
                def _(ve):
                    for ec in range(4):
                        for ts in range(TT):
                            bi = ec * TT + ts
                            ve.wait_ge(s_pev.h, (ec * 3 + 1) * TT + ts + 1)
                            ve.tensor_tensor(kvu[:, UOFF + ts * 512:UOFF + (ts + 1) * 512],
                                             sig2[:, (bi % 2) * 512:(bi % 2 + 1) * 512],
                                             kvu[:, VOFF + ts * 512:VOFF + (ts + 1) * 512],
                                             ALU.mult).then_inc(s_uev.h, 1)
                            s_uev.n += 1

                done["a1b"] = [(s_spill, 144), (s_sev, 8)]

                if "a2" in phases:
                    s_cc = sem("a2_cc")
                    s_pref = sem("a2_pref")

                    @block.gpsimd
                    def _(g):
                        g.wait_ge(s_spill.h, 144)
                        g.collective_compute(
                            "AllGather", ALU.bypass,
                            replica_groups=[list(range(NCORES))],
                            ins=[s_loc.ap().opt()], outs=[s_all.ap().opt()],
                        ).then_inc(s_cc.h, 1)
                        s_cc.n = 1
                    done["a2cc"] = [(s_cc, 1)]

        if "a1b" in phases:
            st_a1b.close()

        if "a2" in phases:
            if True:
                st_pref = ExitStack()
                prefix = st_pref.enter_context(nc.sbuf_tensor("prefix", [128, 32 * 128], F32))
                s_cc = sems["a2_cc"]
                s_pref = sems["a2_pref"]
                st_g = ExitStack()
                sgath = st_g.enter_context(nc.sbuf_tensor("sgath", [128, 2 * 32 * 128], F32))
                if True:
                    s_gld = sem("a2_gld")

                    @block.gpsimd
                    def _(g):
                        g.wait_ge(s_cc.h, 1)
                        for c in range(NCORES):
                            if c >= 2:
                                g.wait_ge(s_pref.h, c - 1)
                            src = s_all.ap()[c * 4096:(c + 1) * 4096, :].rearrange(
                                "(bh e) d -> e bh d", e=128)
                            g.dma_start(out=sgath[:, (c % 2) * 4096:(c % 2 + 1) * 4096]
                                        .rearrange("p (bh d) -> p bh d", d=128),
                                        in_=src).then_inc(s_gld.h, 16)
                            s_gld.n += 16

                    @block.vector
                    def _(ve):
                        ve.wait_ge(sems["a1b_spill"].h, 144)
                        ve.memset(prefix[:, :], 0.0)
                        ve.drain()
                        for c in range(NCORES):
                            ve.wait_ge(s_gld.h, (c + 1) * 16)
                            ve.scalar_tensor_tensor(
                                prefix[:, :], sgath[:, (c % 2) * 4096:(c % 2 + 1) * 4096],
                                pmask_s[:, c:c + 1], prefix[:, :],
                                ALU.mult, ALU.add)
                            ve.drain()
                            ve.nop().then_inc(s_pref.h, 1)
                            s_pref.n += 1
                    done["a2"] = [(s_pref, NCORES)]

                if "a3a" in phases:
                    st_g.close()
                    st_a3a = ExitStack()
                    qsl = st_a3a.enter_context(nc.sbuf_tensor("qsl", [128, 4 * 512], BF16))
                    wqsl = st_a3a.enter_context(nc.sbuf_tensor("wqsl", [128, 2 * 16 * 128], BF16))
                    if True:
                        s_qwld = sem("a3_qwld")
                        s_qmm = sem("a3_qmm")
                        s_qev = sem("a3_qev")
                        s_qst = sem("a3_qst")

                        @block.gpsimd
                        def _(g):
                            g.wait_ge(sems["a2_pref"].h, NCORES)
                            for et in range(16):
                                if et >= 2:
                                    g.wait_ge(s_qmm.h, (et - 1) * NTC)
                                src = wq_in.rearrange("(dt p) e -> p dt e", p=128)[
                                    :, :, et * 128:(et + 1) * 128]
                                g.dma_start(out=wqsl[:, (et % 2) * 2048:(et % 2 + 1) * 2048]
                                            .rearrange("p (dt e) -> p dt e", e=128),
                                            in_=src).then_inc(s_qwld.h, 16)
                                s_qwld.n += 16

                        @block.tensor
                        def _(t):
                            for et in range(16):
                                t.wait_ge(s_qwld.h, (et + 1) * 16)
                                for tc in range(NTC):
                                    gi = et * NTC + tc
                                    pb = PS[gi % 2]
                                    if gi >= 2:
                                        t.wait_ge(s_qev.h, gi - 1)
                                    for dt in range(16):
                                        mm = t.matmul(
                                            pb[:, :],
                                            wqsl[:, (et % 2) * 2048 + dt * 128:
                                                 (et % 2) * 2048 + (dt + 1) * 128],
                                            bufA[:, dt * T + tc * 512: dt * T + (tc + 1) * 512],
                                            start=(dt == 0), stop=(dt == 15))
                                    mm.then_inc(s_qmm.h, 1)
                                    s_qmm.n += 1

                        @block.scalar
                        def _(sc):
                            for et in range(16):
                                for tc in range(NTC):
                                    gi = et * NTC + tc
                                    sc.wait_ge(s_qmm.h, gi + 1)
                                    if gi >= 4:
                                        sc.wait_ge(s_qst.h, (gi - 3) * 16)
                                    sc.activation(
                                        qsl[:, (gi % 4) * 512:(gi % 4 + 1) * 512],
                                        PS[gi % 2][:, :], AF.Copy).then_inc(s_qev.h, 1)
                                    s_qev.n += 1

                        @block.sync
                        def _(sy):
                            for et in range(16):
                                for tc in range(NTC):
                                    gi = et * NTC + tc
                                    sy.wait_ge(s_qev.h, gi + 1)
                                    sy.dma_start(out=q_d[et * 128:(et + 1) * 128,
                                                         tc * 512:(tc + 1) * 512],
                                                 in_=qsl[:, (gi % 4) * 512:(gi % 4 + 1) * 512]
                                                 ).then_inc(s_qst.h, 16)
                                    s_qst.n += 16
                        done["a3a"] = [(s_qst, 16 * NTC * 16)]

                if "a3b" in phases:
                    st_a3a.close()
                    st_a3b = ExitStack()
                    kfm = st_a3b.enter_context(nc.sbuf_tensor("kfm", [128, 16 * TB], BF16))
                    chbuf = st_a3b.enter_context(nc.sbuf_tensor("chbuf", [128, 2 * 3 * 2048], BF16))
                    state_bf = st_a3b.enter_context(nc.sbuf_tensor("state_bf", [128, 16 * 128], BF16))
                    att_sb = st_a3b.enter_context(nc.sbuf_tensor("att_sb", [128, 16 * 128], BF16))
                    if True:
                        s_kfm = sem("a3_kfm")
                        s_chld = sem("a3_chld")
                        s_att = sem("a3_att")
                        s_mask = sem("a3_mask")
                        s_omm = sem("a3_omm")
                        s_oev = sem("a3_oev")
                        s_inc = sem("a3_inc")
                        s_sevac = sem("a3_sevac")
                        s_qst = sems["a3_qst"]
                        s_spill = sems["a1b_spill"]
                        s_pref = sems["a2_pref"]
                        CH3 = 3 * 2048

                        inc_cnt = {}
                        omm_cnt = {}

                        @block.sync
                        def _(sy):
                            sy.wait_ge(s_spill.h, 128)
                            sy.wait_ge(s_qst.h, 16 * NTC * 16)
                            for half in range(2):
                                if half == 1:
                                    sy.wait_ge(s_att.h, NCHB * 4)
                                for et in range(16):
                                    sy.dma_start(out=kfm[:, et * TB:(et + 1) * TB],
                                                 in_=k_d[half * TB:(half + 1) * TB,
                                                         et * 128:(et + 1) * 128],
                                                 transpose=True).then_inc(s_kfm.h, 16)
                                    s_kfm.n += 16

                        @block.scalar
                        def _(sc):
                            sc.wait_ge(s_spill.h, 128)
                            sc.wait_ge(s_qst.h, 16 * NTC * 16)
                            for ch in range(2):
                                off = (ch % 2) * CH3
                                sc.dma_start(out=chbuf[:, off:off + 2048],
                                             in_=k_d[ch * 128:(ch + 1) * 128, :]).then_inc(s_chld.h, 16)
                                sc.dma_start(out=chbuf[:, off + 2048:off + 4096],
                                             in_=u_d[ch * 128:(ch + 1) * 128, :]).then_inc(s_chld.h, 16)
                                sc.dma_start(out=chbuf[:, off + 4096:off + 6144]
                                             .rearrange("p (et t) -> p et t", t=128),
                                             in_=q_dv[:, :, ch * 128:(ch + 1) * 128]).then_inc(s_chld.h, 16)
                                s_chld.n += 48

                        if "scanprobe" in dbg:
                            s_sprobe = sem("a3_sprobe")

                            @block.gpsimd
                            def _(g):
                                g.wait_ge(s_oev.h, NH)
                                g.dma_start(out=sp_dbg[:, 0:2048], in_=att_sb[:, :]).then_inc(s_sprobe.h, 16)
                                g.dma_start(out=sp_dbg[:, 2048:8192], in_=chbuf[:, 0:6144]).then_inc(s_sprobe.h, 16)
                                g.dma_start(out=sp_dbg[:, 8192:10240], in_=state_bf[:, :]).then_inc(s_sprobe.h, 16)
                                g.dma_start(out=sp_dbg[:, 10240:10240 + 16 * TB], in_=kfm[:, :]).then_inc(s_sprobe.h, 16)
                                s_sprobe.n = 64

                        @block.tensor
                        def _(t):
                            t.wait_ge(s_pref.h, NCORES)
                            t.wait_ge(s_const.h, 80)
                            for ch in range(NCH):
                                if ch == 1 and "scanprobe" in dbg:
                                    t.wait_ge(sems["a3_sprobe"].h, 64)
                                half = 0 if ch < NCHB else 1
                                first = (ch % NCHB == 0)
                                t.wait_ge(s_chld.h, (ch + 1) * 48)
                                t.wait_ge(s_kfm.h, (half + 1) * 256)
                                off = (ch % 2) * CH3
                                tloc = (ch % NCHB) * 128
                                if first:
                                    if half == 1:
                                        t.wait_ge(s_sevac.h, NCHB * NH)
                                    for h in range(NH):
                                        bh = half * 16 + h
                                        pb = PS[4 + h // 4]
                                        mm = t.matmul(pb[:, (h % 4) * 128:(h % 4 + 1) * 128],
                                                      ident_s[:, :],
                                                      prefix[:, bh * 128:(bh + 1) * 128],
                                                      start=(h % 4 == 0), stop=False,
                                                      skip_group_check=True)
                                    mm.then_inc(s_inc.h, 1)
                                    s_inc.n += 1
                                for g4 in range(4):
                                    AG = ch * 4 + g4
                                    pb_att = PS[g4 % 2]
                                    if AG >= 2:
                                        t.wait_ge(s_mask.h, (AG - 1) * 4)
                                    for hh in range(4):
                                        h = g4 * 4 + hh
                                        mm = t.matmul(pb_att[:, hh * 128:(hh + 1) * 128],
                                                      kfm[:, h * TB + tloc: h * TB + tloc + 128],
                                                      chbuf[:, off + 4096 + h * 128:
                                                            off + 4096 + (h + 1) * 128],
                                                      start=True, stop=True)
                                    mm.then_inc(s_att.h, 1)
                                    s_att.n += 1
                                for h in range(NH):
                                    gidx = ch * NH + h
                                    g4 = h // 4
                                    OG = ch * 4 + g4
                                    pb_o = PS[2 + g4 % 2]
                                    oslot = h % 8
                                    if h % 4 == 0 and OG >= 2:
                                        t.wait_ge(s_oev.h, (OG - 1) * 4)
                                    t.wait_ge(s_mask.h, gidx + 1)
                                    t.wait_ge(s_sevac.h, gidx + 1)
                                    t.matmul(pb_o[:, (h % 4) * 128:(h % 4 + 1) * 128],
                                             chbuf[:, off + 2048 + h * 128:
                                                   off + 2048 + (h + 1) * 128],
                                             att_sb[:, h * 128:(h + 1) * 128],
                                             start=True, stop=False)
                                    mm = t.matmul(pb_o[:, (h % 4) * 128:(h % 4 + 1) * 128],
                                                  state_bf[:, h * 128:(h + 1) * 128],
                                                  chbuf[:, off + 4096 + h * 128:
                                                        off + 4096 + (h + 1) * 128],
                                                  start=False, stop=True)
                                    mm.then_inc(s_omm.h, 1)
                                    s_omm.n += 1
                                    omm_cnt[(ch, h)] = s_omm.n
                                    pb_s = PS[4 + h // 4]
                                    t.wait_ge(s_sevac.h, ch * NH + (h // 4 + 1) * 4)
                                    mm = t.matmul(pb_s[:, (h % 4) * 128:(h % 4 + 1) * 128],
                                                  chbuf[:, off + h * 128: off + (h + 1) * 128],
                                                  chbuf[:, off + 2048 + h * 128:
                                                        off + 2048 + (h + 1) * 128],
                                                  start=False,
                                                  stop=(ch % NCHB == NCHB - 1 and h % 4 == 3),
                                                  skip_group_check=True)
                                    mm.then_inc(s_inc.h, 1)
                                    s_inc.n += 1
                                    inc_cnt[(ch, h)] = s_inc.n

                        @block.vector
                        def _(ve):
                            for ch in range(NCH):
                                if ch == 1 and "scanprobe" in dbg:
                                    ve.wait_ge(sems["a3_sprobe"].h, 64)
                                for h in range(NH):
                                    gidx = ch * NH + h
                                    g4 = h // 4
                                    AG = ch * 4 + g4
                                    pb_att = PS[g4 % 2]
                                    ve.wait_ge(s_att.h, AG + 1)
                                    if ch >= 1:
                                        ve.wait_ge(s_omm.h, (ch - 1) * NH + h + 1)
                                    ve.tensor_tensor(
                                        att_sb[:, h * 128:(h + 1) * 128],
                                        pb_att[:, (h % 4) * 128:(h % 4 + 1) * 128],
                                        maskT_s[:, :], ALU.mult).then_inc(s_mask.h, 1)
                                    s_mask.n += 1

                        @block.scalar
                        def _(sc):
                            for ch in range(NCH):
                                if ch == 1 and "scanprobe" in dbg:
                                    sc.wait_ge(sems["a3_sprobe"].h, 64)
                                half = 0 if ch < NCHB else 1
                                first = (ch % NCHB == 0)
                                for h in range(NH):
                                    pb_s = PS[4 + h // 4]
                                    if first:
                                        sc.wait_ge(s_inc.h, 1 if half == 0
                                                   else NCHB * NH + 2)
                                        if half == 1:
                                            sc.wait_ge(s_omm.h, omm_cnt[(NCHB - 1, h)])
                                    else:
                                        sc.wait_ge(s_inc.h, inc_cnt[(ch - 1, (h // 4) * 4 + 3)])
                                    sc.activation(state_bf[:, h * 128:(h + 1) * 128],
                                                  pb_s[:, (h % 4) * 128:(h % 4 + 1) * 128],
                                                  AF.Copy).then_inc(s_sevac.h, 1)
                                    s_sevac.n += 1
                                for h in range(NH):
                                    gidx = ch * NH + h
                                    g4 = h // 4
                                    pb_o = PS[2 + g4 % 2]
                                    sc.wait_ge(s_omm.h, omm_cnt[(ch, g4 * 4 + 3)])
                                    sc.activation(bufA[:, h * T + ch * 128: h * T + (ch + 1) * 128],
                                                  pb_o[:, (h % 4) * 128:(h % 4 + 1) * 128],
                                                  AF.Copy).then_inc(s_oev.h, 1)
                                    s_oev.n += 1
                                if ch + 2 < NCH:
                                    ch2 = ch + 2
                                    sc.wait_ge(s_inc.h, inc_cnt[(ch, NH - 1)])
                                    off2 = (ch2 % 2) * CH3
                                    sc.dma_start(out=chbuf[:, off2:off2 + 2048],
                                                 in_=k_d[ch2 * 128:(ch2 + 1) * 128, :]).then_inc(s_chld.h, 16)
                                    sc.dma_start(out=chbuf[:, off2 + 2048:off2 + 4096],
                                                 in_=u_d[ch2 * 128:(ch2 + 1) * 128, :]).then_inc(s_chld.h, 16)
                                    sc.dma_start(out=chbuf[:, off2 + 4096:off2 + 6144]
                                                 .rearrange("p (et t) -> p et t", t=128),
                                                 in_=q_dv[:, :, ch2 * 128:(ch2 + 1) * 128]).then_inc(s_chld.h, 16)
                                    s_chld.n += 48
                        done["a3b"] = [(s_oev, NCH * NH)]

                if "a3c" in phases:
                    st_a3b.close()
                    st_pref.close()
                    st_a3c = ExitStack()
                    wosl = st_a3c.enter_context(nc.sbuf_tensor("wosl", [128, 16 * 2048], BF16))
                    ln_out = st_a3c.enter_context(nc.sbuf_tensor("ln_out", [128, 3 * D], F32))
                    if True:
                        s_wold = sem("a3c_wold")
                        s_amm = sem("a3c_amm")
                        s_aev = sem("a3c_aev")
                        s_xld = sem("a3c_xld")
                        sv2 = sem("a3c_sv")
                        sa2 = sem("a3c_sa")
                        s_x2st = sem("a3c_x2st")

                        @block.gpsimd
                        def _(g):
                            g.wait_ge(sems["a3_qmm"].h, 16 * NTC)
                            g.wait_ge(sems["a3_inc"].h, NCH * NH + 2)
                            g.wait_ge(sems["a3_sevac"].h, NCH * NH)
                            if "o_fm" in dbg:
                                s_odbg = sem("a3c_odbg")
                                g.wait_ge(sems["a3_oev"].h, NCH * NH)
                                g.dma_start(out=o_dbg, in_=bufA[:, :16 * T]).then_inc(s_odbg.h, 16)
                                s_odbg.n = 16
                                g.wait_ge(s_odbg.h, 16)
                            g.dma_start(out=wosl[:, :].rearrange("p (et e) -> p et e", e=2048),
                                        in_=wo_in.rearrange("(et p) e -> p et e", p=128)
                                        ).then_inc(s_wold.h, 16)
                            s_wold.n = 16
                            for ts in range(TT):
                                if ts >= 2:
                                    g.wait_ge(sv2.h, (ts - 1) * 4)
                                g.dma_start(out=xio[:, (ts % 2) * D:(ts % 2 + 1) * D],
                                            in_=x_in[ts * 128:(ts + 1) * 128, :]).then_inc(s_xld.h, 16)
                                s_xld.n += 16

                        @block.tensor
                        def _(t):
                            t.wait_ge(s_wold.h, 16)
                            t.wait_ge(sems["a3_oev"].h, NCH * NH)
                            for ts in range(TT):
                                for dc in range(4):
                                    gi = ts * 4 + dc
                                    pb = PS[gi % 2]
                                    if gi >= 2:
                                        t.wait_ge(s_aev.h, gi - 1)
                                    for h in range(NH):
                                        mm = t.matmul(
                                            pb[:, :],
                                            bufA[:, h * T + ts * 128: h * T + (ts + 1) * 128],
                                            wosl[:, h * 2048 + dc * 512: h * 2048 + (dc + 1) * 512],
                                            start=(h == 0), stop=(h == NH - 1))
                                    mm.then_inc(s_amm.h, 1)
                                    s_amm.n += 1

                        @block.vector
                        def _(ve):
                            for ts in range(TT):
                                for dc in range(4):
                                    gi = ts * 4 + dc
                                    ve.wait_ge(s_amm.h, gi + 1)
                                    ins = ve.tensor_copy(ln_out[:, dc * 512:(dc + 1) * 512],
                                                         PS[gi % 2][:, :])
                                    ins.then_inc(s_aev.h, 1)
                                    s_aev.n += 1
                                ve.drain()

                                def extra(ve, ts=ts):
                                    ve.wait_ge(s_xld.h, (ts + 1) * 16)
                                    if ts >= 1:
                                        ve.wait_ge(s_x2st.h, ts * 16)
                                    ve.tensor_tensor(ln_out[:, 2 * D:3 * D],
                                                     ln_out[:, D:2 * D],
                                                     xio[:, (ts % 2) * D:(ts % 2 + 1) * D],
                                                     ALU.add).then_inc(sv2.h, 1)
                                ln_tile_vec(ve, sv2, sa2, ts, ln_out[:, 0:D], extra)

                        @block.scalar
                        def _(sc):
                            for ts in range(TT):
                                sc.wait_ge(s_aev.h, ts * 4 + 4)
                                ln_tile_act(sc, sv2, sa2, ts, ln_out[:, 0:D],
                                            ln_out[:, D:2 * D])

                        @block.sync
                        def _(sy):
                            for ts in range(TT):
                                sy.wait_ge(sv2.h, (ts + 1) * 4)
                                sy.dma_start(out=x2_d[ts * 128:(ts + 1) * 128, :],
                                             in_=ln_out[:, 2 * D:3 * D]).then_inc(s_x2st.h, 16)
                                s_x2st.n += 16
                        done["a3c"] = [(s_x2st, TT * 16)]

        if "a2" in phases:
            if "a3c" in phases:
                st_a3c.close()
            elif "a3b" in phases:
                st_a3b.close()
                st_pref.close()
            elif "a3a" in phases:
                st_a3a.close()
                st_pref.close()
            else:
                st_g.close()
                st_pref.close()

        if "bprep" in phases:
            if True:
                st_bp = ExitStack()
                mix_s = st_bp.enter_context(nc.sbuf_tensor("mix_s", [128, 2 * 4096], BF16))
                tbuf = st_bp.enter_context(nc.sbuf_tensor("tbuf", [128, 2 * 8192], BF16))
                s_wcast = sem("bp_wcast")
                s_mixld = sem("bp_mixld")
                s_mixmm = sem("bp_mixmm")
                s_mixev = sem("bp_mixev")
                s_mixst = sem("bp_mixst")
                s_tp = sem("bp_tp")
                s_tst = sem("bp_tst")

                @block.gpsimd
                def _(g):
                    g.wait_ge(s_const.h, 80)
                    if "a3c" in phases:
                        g.wait_ge(sems["a3c_amm"].h, 4 * TT)
                        g.wait_ge(sems["a3c_x2st"].h, TT * 16)
                    g.dma_start(out=w1b, in_=w1_in).then_inc(s_wcast.h, 16)
                    g.dma_start(out=w3b, in_=w3_in).then_inc(s_wcast.h, 16)
                    s_wcast.n = 32
                    for j in range(64):
                        if j >= 2:
                            g.wait_ge(s_mixmm.h, (j - 1) * 8)
                        src = w2_in.rearrange("d (a b) -> a d b", b=128)[
                            :, j * 32:(j + 1) * 32, :]
                        g.dma_start(out=mix_s[0:64, (j % 2) * 4096:(j % 2 + 1) * 4096]
                                    .rearrange("p (dd b) -> p dd b", b=128),
                                    in_=src).then_inc(s_mixld.h, 16)
                        s_mixld.n += 16
                        g.wait_ge(s_mixev.h, (j + 1) * 8)
                        dst = w2hb.rearrange("d (a b) -> a d b", b=128)[
                            :, j * 32:(j + 1) * 32, :]
                        g.dma_start(out=dst,
                                    in_=mix_s[64:128, (j % 2) * 4096:(j % 2 + 1) * 4096]
                                    .rearrange("p (dd b) -> p dd b", b=128)
                                    ).then_inc(s_mixst.h, 16)
                        s_mixst.n += 16

                @block.tensor
                def _(t):
                    for j in range(64):
                        t.wait_ge(s_mixld.h, (j + 1) * 16)
                        for jj in range(8):
                            gi = j * 8 + jj
                            pb = PS[gi % 2]
                            if gi >= 2:
                                t.wait_ge(s_mixev.h, gi - 1)
                            mm = t.matmul(pb[0:64, :], h64_s[:, :],
                                          mix_s[0:64, (j % 2) * 4096 + jj * 512:
                                                (j % 2) * 4096 + (jj + 1) * 512],
                                          start=True, stop=True)
                            mm.then_inc(s_mixmm.h, 1)
                            s_mixmm.n += 1

                @block.vector
                def _(ve):
                    for j in range(64):
                        for jj in range(8):
                            gi = j * 8 + jj
                            ve.wait_ge(s_mixmm.h, gi + 1)
                            if jj == 0 and j >= 2:
                                ve.wait_ge(s_mixst.h, (j - 1) * 16)
                            ins = ve.tensor_copy(
                                mix_s[64:128, (j % 2) * 4096 + jj * 512:
                                      (j % 2) * 4096 + (jj + 1) * 512],
                                PS[gi % 2][0:64, :])
                            ins.then_inc(s_mixev.h, 1)
                            s_mixev.n += 1

                TP = ([("w1", w1b, w1tb, HID, dsl) for dsl in range(16)] +
                      [("w3", w3b, w3tb, HID, dsl) for dsl in range(16)] +
                      [("w2h", w2hb, w2htb, D, hs) for hs in range(64)])

                @block.sync
                def _(sy):
                    if "a3c" in phases:
                        sy.wait_ge(sems["a3c_amm"].h, 4 * TT)
                        sy.wait_ge(sems["a3c_x2st"].h, TT * 16)
                    for i, (tag, srcb, dstb, rows, sl) in enumerate(TP):
                        if tag == "w1":
                            sy.wait_ge(s_wcast.h, 16)
                        elif tag == "w3":
                            sy.wait_ge(s_wcast.h, 32)
                        else:
                            sy.wait_ge(s_mixst.h, 64 * 16)
                        if i >= 2:
                            sy.wait_ge(s_tst.h, (i - 1) * 16)
                        sy.dma_start(out=tbuf[:, (i % 2) * 8192:(i % 2) * 8192 + rows],
                                     in_=srcb[0:rows, sl * 128:(sl + 1) * 128],
                                     transpose=True).then_inc(s_tp.h, 16)
                        s_tp.n += 16

                @block.scalar
                def _(sc):
                    for i, (tag, srcb, dstb, rows, sl) in enumerate(TP):
                        sc.wait_ge(s_tp.h, (i + 1) * 16)
                        sc.dma_start(out=dstb[sl * 128:(sl + 1) * 128, 0:rows],
                                     in_=tbuf[:, (i % 2) * 8192:(i % 2) * 8192 + rows]
                                     ).then_inc(s_tst.h, 16)
                        s_tst.n += 16
                done["bprep"] = [(s_tst, len(TP) * 16)]

        if "bprep" in phases:
            st_bp.close()

        if "b" in phases:
            done["b_ln3"] = ln_batched("ln3", x2_d, xn2_d,
                                       [(sems["a3c_x2st"], TT * 16)] if "a3c" in phases else [])
            NTCB = min(NTC, int(os.environ.get("KBLIM", "99")))
            st_b = ExitStack()
            xn2fm = st_b.enter_context(nc.sbuf_tensor("xn2fm", [128, 16 * 512], BF16))
            w13sl = st_b.enter_context(nc.sbuf_tensor("w13sl", [128, 4 * 2048], BF16))
            w2sl = st_b.enter_context(nc.sbuf_tensor("w2sl", [128, 8 * 1024], BF16))
            gbuf = st_b.enter_context(nc.sbuf_tensor("gbuf", [128, 2 * 512], BF16))
            silu_s = st_b.enter_context(nc.sbuf_tensor("silu_s", [128, 2 * 512], F32))
            ffn_sb = st_b.enter_context(nc.sbuf_tensor("ffn_sb", [128, 4 * 2048], F32))
            lnb_out = st_b.enter_context(nc.sbuf_tensor("lnb_out", [128, 2 * D], F32))
            if True:
                s_xn2 = sem("b_xn2")
                s_w13 = sem("b_w13")
                s_gmm = sem("b_gmm")
                s_silu = sem("b_silu")
                s_gtt = sem("b_gtt")
                s_ymm = sem("b_ymm")
                s_yev = sem("b_yev")
                s_w2ld = sem("b_w2ld")
                s_dmm = sem("b_dmm")
                s_dev = sem("b_dev")
                s_x2ld = sem("b_x2ld")
                svb = sem("b_sv")
                sab = sem("b_sa")
                s_out = sem("b_out")
                s_gprog = sem("b_gprog")

                ln3_st = sems["ln3_st"]

                @block.sync
                def _(sy):
                    sy.wait_ge(ln3_st.h, TT * 16)
                    if "bprep" in phases:
                        sy.wait_ge(sems["bp_tst"].h, 96 * 16)

                    def emit_stores(tc):
                        for ts4 in range(4):
                            i = tc * 4 + ts4
                            sy.wait_ge(svb.h, (i + 1) * 4)
                            sy.dma_start(out=out[i * 128:(i + 1) * 128, :],
                                         in_=lnb_out[:, D:2 * D]).then_inc(s_out.h, 16)
                            s_out.n += 16
                    for tc in range(NTCB):
                        if tc >= 1:
                            sy.wait_ge(s_gmm.h, tc * 64)
                        for dsl in range(16):
                            sy.dma_start(out=xn2fm[:, dsl * 512:(dsl + 1) * 512],
                                         in_=xn2_d[tc * 512:(tc + 1) * 512,
                                                   dsl * 128:(dsl + 1) * 128],
                                         transpose=True).then_inc(s_xn2.h, 16)
                            s_xn2.n += 16
                        if tc >= 1:
                            emit_stores(tc - 1)
                    emit_stores(NTCB - 1)

                @block.gpsimd
                def _(g):
                    if "bprep" in phases:
                        g.wait_ge(sems["bp_tst"].h, 96 * 16)
                    for tc in range(NTCB):
                        for a in range(64):
                            i = tc * 64 + a
                            if i >= 2:
                                g.wait_ge(s_gmm.h, i - 1)
                            g.dma_start(out=w13sl[:, (a % 2) * 2048:(a % 2 + 1) * 2048]
                                        .rearrange("p (dt h) -> p dt h", h=128),
                                        in_=w1tb.rearrange("(dt p) h -> p dt h", p=128)[
                                            :, :, a * 128:(a + 1) * 128]).then_inc(s_w13.h, 16)
                            g.dma_start(out=w13sl[:, 4096 + (a % 2) * 2048:
                                                  4096 + (a % 2 + 1) * 2048]
                                        .rearrange("p (dt h) -> p dt h", h=128),
                                        in_=w3tb.rearrange("(dt p) h -> p dt h", p=128)[
                                            :, :, a * 128:(a + 1) * 128]).then_inc(s_w13.h, 16)
                            s_w13.n += 32
                        for half in range(2):
                            for gr in range(8):
                                gi = (tc * 2 + half) * 8 + gr
                                if gi >= 1:
                                    g.wait_ge(s_gprog.h, gi)
                                g.dma_start(out=w2sl[:, :].rearrange("p (ag dd) -> p ag dd", dd=1024),
                                            in_=w2htb.rearrange("(a p) dd -> p a dd", p=128)[
                                                :, gr * 8:(gr + 1) * 8,
                                                half * 1024:(half + 1) * 1024]).then_inc(s_w2ld.h, 16)
                                s_w2ld.n += 16
                        for ts4 in range(4):
                            i = tc * 4 + ts4
                            if i < 2:
                                g.wait_ge(ln3_st.h, TT * 16)
                            else:
                                g.wait_ge(svb.h, (i - 1) * 4)
                            g.dma_start(out=xio[:, (i % 2) * D:(i % 2 + 1) * D],
                                        in_=x2_d[(tc * 4 + ts4) * 128:(tc * 4 + ts4 + 1) * 128, :]
                                        ).then_inc(s_x2ld.h, 16)
                            s_x2ld.n += 16

                @block.tensor
                def _(t):
                    for tc in range(NTCB):
                        t.wait_ge(s_xn2.h, (tc + 1) * 256)
                        for a in range(64):
                            i = tc * 64 + a
                            t.wait_ge(s_w13.h, (i + 1) * 32)
                            pg = PS[2 * (a % 2)]
                            pl = PS[2 * (a % 2) + 1]
                            if a == 0 and tc > 0:
                                t.wait_ge(s_dev.h, tc * 16)
                            if i >= 2:
                                t.wait_ge(s_silu.h, i - 1)
                                t.wait_ge(s_gtt.h, i - 1)
                            for dt in range(16):
                                t.matmul(pg[:, :],
                                         w13sl[:, (a % 2) * 2048 + dt * 128:
                                               (a % 2) * 2048 + (dt + 1) * 128],
                                         xn2fm[:, dt * 512:(dt + 1) * 512],
                                         start=(dt == 0), stop=(dt == 15))
                            for dt in range(16):
                                mm = t.matmul(pl[:, :],
                                              w13sl[:, 4096 + (a % 2) * 2048 + dt * 128:
                                                    4096 + (a % 2) * 2048 + (dt + 1) * 128],
                                              xn2fm[:, dt * 512:(dt + 1) * 512],
                                              start=(dt == 0), stop=(dt == 15))
                            mm.then_inc(s_gmm.h, 1)
                            s_gmm.n += 1
                            t.wait_ge(s_gtt.h, i + 1)
                            py = PS[4 + (a % 2)]
                            if i >= 2:
                                t.wait_ge(s_yev.h, i - 1)
                            mm = t.matmul(py[:, :], h128_s[:, :],
                                          gbuf[:, (a % 2) * 512:(a % 2 + 1) * 512],
                                          start=True, stop=True)
                            mm.then_inc(s_ymm.h, 1)
                            s_ymm.n += 1
                        t.wait_ge(s_yev.h, (tc + 1) * 64)
                        for half in range(2):
                            for gr in range(8):
                                gi = (tc * 2 + half) * 8 + gr
                                t.wait_ge(s_w2ld.h, (gi + 1) * 16)
                                if gr == 0 and (tc > 0 or half > 0):
                                    t.wait_ge(s_dev.h, (tc * 2 + half) * 8)
                                for ag in range(8):
                                    a = gr * 8 + ag
                                    for pair in range(8):
                                        ts4, dc = pair // 2, pair % 2
                                        mm = t.matmul(
                                            PS[pair][:, :],
                                            bufA[:, a * 512 + ts4 * 128: a * 512 + (ts4 + 1) * 128],
                                            w2sl[:, ag * 1024 + dc * 512: ag * 1024 + (dc + 1) * 512],
                                            start=(a == 0), stop=(a == 63))
                                mm.then_inc(s_gprog.h, 1)
                                s_gprog.n += 1

                @block.scalar
                def _(sc):
                    for tc in range(NTCB):
                        for a in range(64):
                            i = tc * 64 + a
                            sc.wait_ge(s_gmm.h, i + 1)
                            pg = PS[2 * (a % 2)]
                            if i >= 2:
                                sc.wait_ge(s_gtt.h, i - 1)
                            sc.activation(silu_s[:, (a % 2) * 512:(a % 2 + 1) * 512],
                                          pg[:, :], AF.Silu).then_inc(s_silu.h, 1)
                            s_silu.n += 1
                        for ts4 in range(4):
                            i = tc * 4 + ts4
                            sc.wait_ge(s_dev.h, (tc + 1) * 16)
                            ln_tile_act(sc, svb, sab, i,
                                        ffn_sb[:, ts4 * 2048:(ts4 + 1) * 2048],
                                        lnb_out[:, 0:D])

                @block.vector
                def _(ve):
                    for tc in range(NTCB):
                        for a in range(64):
                            i = tc * 64 + a
                            ve.wait_ge(s_silu.h, i + 1)
                            pl = PS[2 * (a % 2) + 1]
                            if i >= 2:
                                ve.wait_ge(s_ymm.h, i - 1)
                            ve.tensor_tensor(gbuf[:, (a % 2) * 512:(a % 2 + 1) * 512],
                                             silu_s[:, (a % 2) * 512:(a % 2 + 1) * 512],
                                             pl[:, :], ALU.mult).then_inc(s_gtt.h, 1)
                            s_gtt.n += 1
                            ve.wait_ge(s_ymm.h, i + 1)
                            py = PS[4 + (a % 2)]
                            ve.tensor_scalar(bufA[:, a * 512:(a + 1) * 512],
                                             py[:, :], RSQRT_N, None,
                                             ALU.mult).then_inc(s_yev.h, 1)
                            s_yev.n += 1
                        for half in range(2):
                            ve.wait_ge(s_gprog.h, (tc * 2 + half) * 8 + 8)
                            for pair in range(8):
                                ts4, dc = pair // 2, pair % 2
                                ins = ve.tensor_copy(
                                    ffn_sb[:, ts4 * 2048 + half * 1024 + dc * 512:
                                           ts4 * 2048 + half * 1024 + (dc + 1) * 512],
                                    PS[pair][:, :])
                                ins.then_inc(s_dev.h, 1)
                                s_dev.n += 1
                        ve.drain()
                        for ts4 in range(4):
                            i = tc * 4 + ts4

                            def extra(ve, i=i, ts4=ts4, tc=tc):
                                ve.wait_ge(s_x2ld.h, (i + 1) * 16)
                                if i >= 1:
                                    ve.wait_ge(s_out.h, i * 16)
                                ve.tensor_tensor(lnb_out[:, D:2 * D],
                                                 lnb_out[:, 0:D],
                                                 xio[:, (i % 2) * D:(i % 2 + 1) * D],
                                                 ALU.add).then_inc(svb.h, 1)
                            ln_tile_vec(ve, svb, sab, i,
                                        ffn_sb[:, ts4 * 2048:(ts4 + 1) * 2048], extra)

                done["b"] = [(s_out, NTCB * 4 * 16)]

        if "b" in phases:
            st_b.close()

        s_fin = sem("final")

        @block.gpsimd
        def _(g):
            for lst in done.values():
                for sm, cn in lst:
                    g.wait_ge(sm.h, cn)
            for n, dap in dbg.items():
                g.dma_start(out=dap, in_=dbg_specs[n][0]).then_inc(s_fin.h, 16)
                s_fin.n += 16
            if s_fin.n:
                g.wait_ge(s_fin.h, s_fin.n)

    return nc


def make_consts():
    def hadamard(n):
        Hm = np.array([[1.0]], dtype=np.float64)
        while Hm.shape[0] < n:
            Hm = np.block([[Hm, Hm], [Hm, -Hm]])
        return Hm
    h128 = hadamard(128).astype(ml_dtypes.bfloat16)
    h64 = hadamard(64).astype(ml_dtypes.bfloat16)
    maskT = np.triu(np.ones((128, 128))).astype(ml_dtypes.bfloat16)
    ident = np.eye(128, dtype=np.float32)
    return h128, h64, maskT, ident


def kernel(L0_attn_hidden_states, L0_attn_Wq, L0_attn_Wk, L0_attn_Wv,
           L0_attn_Wb, L0_attn_Wo, L0_ffn_W1, L0_ffn_W2, L0_ffn_W3):
    hs = np.ascontiguousarray(np.asarray(L0_attn_hidden_states, dtype=np.float32))
    B, S, _ = hs.shape
    T = 2048
    SC = S // NCORES
    h128, h64, maskT, ident = make_consts()
    weights = {
        "wq": np.ascontiguousarray(np.asarray(L0_attn_Wq, dtype=np.float32)),
        "wk": np.ascontiguousarray(np.asarray(L0_attn_Wk, dtype=np.float32)),
        "wv": np.ascontiguousarray(np.asarray(L0_attn_Wv, dtype=np.float32)),
        "wbeta": np.ascontiguousarray(np.asarray(L0_attn_Wb, dtype=np.float32)),
        "wo": np.ascontiguousarray(np.asarray(L0_attn_Wo, dtype=np.float32)),
        "w1": np.ascontiguousarray(np.asarray(L0_ffn_W1, dtype=np.float32)),
        "w2": np.ascontiguousarray(np.asarray(L0_ffn_W2, dtype=np.float32)),
        "w3": np.ascontiguousarray(np.asarray(L0_ffn_W3, dtype=np.float32)),
    }
    nc = build_program(T=T)
    in_maps = []
    for c in range(NCORES):
        pm = np.zeros((128, 8), dtype=np.float32)
        pm[:, :c] = 1.0
        xc = np.ascontiguousarray(hs[:, c * SC:(c + 1) * SC, :].reshape(T, D))
        m = {"x": xc, "h128": h128, "h64": h64, "maskT": maskT,
             "ident": ident, "pmask": pm}
        m.update(weights)
        in_maps.append(m)
    res = run_bass_kernel_spmd(nc, in_maps, core_ids=list(range(NCORES)))
    shards = [res.results[c]["out"].reshape(B, SC, D) for c in range(NCORES)]
    return np.concatenate(shards, axis=1)


# version 03 (final)
# speedup vs baseline: 1.2195x; 1.2195x over previous
import os
from contextlib import ExitStack

import numpy as np
import ml_dtypes
import concourse.bass as bass
import concourse.mybir as mybir
from concourse.bass_utils import run_bass_kernel_spmd

F32 = mybir.dt.float32
BF16 = mybir.dt.bfloat16
AF = mybir.ActivationFunctionType
ALU = mybir.AluOpType
AX = mybir.AxisListType

D = 2048
HID = 8192
NH = 16
NCORES = 8
EPS = 1e-5
RSQRT_N = float(1.0 / np.sqrt(HID))

ALL_PHASES = ("a1a", "a1b", "a2", "a3a", "a3b", "a3c", "bprep", "b")


class Sem:
    def __init__(self, nc, name):
        self.h = nc.alloc_semaphore(name=name)
        self.n = 0

    def inc(self, instr, k=1):
        instr.then_inc(self.h, k)
        self.n += k
        return self.n


def build_program(T=2048, phases=ALL_PHASES, debug_dump=()):
    assert T % 1024 == 0
    TT = T // 128
    NCH = T // 128
    NCHB = NCH // 2
    NTC = T // 512
    TB = T // 2

    nc = bass.Bass("TRN2", target_bir_lowering=False, debug=False,
                   detect_race_conditions=False)

    x_in = nc.dram_tensor("x", [T, D], F32, kind="ExternalInput").ap()
    wq_in = nc.dram_tensor("wq", [D, D], F32, kind="ExternalInput").ap()
    wk_in = nc.dram_tensor("wk", [D, D], F32, kind="ExternalInput").ap()
    wv_in = nc.dram_tensor("wv", [D, D], F32, kind="ExternalInput").ap()
    wb_in = nc.dram_tensor("wbeta", [D, D], F32, kind="ExternalInput").ap()
    wo_in = nc.dram_tensor("wo", [D, D], F32, kind="ExternalInput").ap()
    w1_in = nc.dram_tensor("w1", [HID, D], F32, kind="ExternalInput").ap()
    w2_in = nc.dram_tensor("w2", [D, HID], F32, kind="ExternalInput").ap()
    w3_in = nc.dram_tensor("w3", [HID, D], F32, kind="ExternalInput").ap()
    h128_in = nc.dram_tensor("h128", [128, 128], BF16, kind="ExternalInput").ap()
    h64_in = nc.dram_tensor("h64", [64, 64], BF16, kind="ExternalInput").ap()
    maskT_in = nc.dram_tensor("maskT", [128, 128], BF16, kind="ExternalInput").ap()
    ident_in = nc.dram_tensor("ident", [128, 128], F32, kind="ExternalInput").ap()
    pmask_in = nc.dram_tensor("pmask", [128, 8], F32, kind="ExternalInput").ap()
    out = nc.dram_tensor("out", [T, D], F32, kind="ExternalOutput").ap()

    xn_d = nc.dram_tensor("xn_d", [T, D], BF16, kind="Internal").ap()
    k_d = nc.dram_tensor("k_d", [T, D], BF16, kind="Internal").ap()
    u_d = nc.dram_tensor("u_d", [T, D], BF16, kind="Internal").ap()
    q_d = nc.dram_tensor("q_d", [D, T], BF16, kind="Internal").ap()
    s_loc = nc.dram_tensor("s_loc", [32 * 128, 128], F32, kind="Internal")
    s_all = nc.dram_tensor("s_all", [NCORES * 32 * 128, 128], F32, kind="Internal")
    x2_d = nc.dram_tensor("x2_d", [T, D], F32, kind="Internal").ap()
    xn2_d = nc.dram_tensor("xn2_d", [T, D], BF16, kind="Internal").ap()
    w1b = nc.dram_tensor("w1b", [HID, D], BF16, kind="Internal").ap()
    w3b = nc.dram_tensor("w3b", [HID, D], BF16, kind="Internal").ap()
    w2hb = nc.dram_tensor("w2hb", [D, HID], BF16, kind="Internal").ap()
    w1tb = nc.dram_tensor("w1tb", [D, HID], BF16, kind="Internal").ap()
    w3tb = nc.dram_tensor("w3tb", [D, HID], BF16, kind="Internal").ap()
    w2htb = nc.dram_tensor("w2htb", [HID, D], BF16, kind="Internal").ap()

    o_dbg = nc.dram_tensor("o_dbg_d", [128, 16 * T], BF16, kind="Internal").ap()
    sp_dbg = nc.dram_tensor("sp_dbg_d", [128, 16 * TB + 2048 + 6144 + 2048], BF16, kind="Internal").ap()
    dbg_specs = {
        "scanprobe": (sp_dbg, [128, 16 * TB + 2048 + 6144 + 2048], BF16),
        "o_fm": (o_dbg, [128, 16 * T], BF16),
        "xn_d": (xn_d, [T, D], BF16), "k_d": (k_d, [T, D], BF16),
        "u_d": (u_d, [T, D], BF16), "q_d": (q_d, [D, T], BF16),
        "s_loc": (s_loc.ap(), [32 * 128, 128], F32),
        "s_all": (s_all.ap(), [NCORES * 32 * 128, 128], F32),
        "x2_d": (x2_d, [T, D], F32), "xn2_d": (xn2_d, [T, D], BF16),
        "w1b": (w1b, [HID, D], BF16), "w2hb": (w2hb, [D, HID], BF16),
        "w1tb": (w1tb, [D, HID], BF16), "w3tb": (w3tb, [D, HID], BF16),
        "w2htb": (w2htb, [HID, D], BF16),
    }
    dbg = {n: nc.dram_tensor("dbg_" + n, dbg_specs[n][1], dbg_specs[n][2],
                             kind="ExternalOutput").ap() for n in debug_dump}

    k_dv = k_d.rearrange("(r p) c -> p r c", p=128)
    u_dv = u_d.rearrange("(r p) c -> p r c", p=128)
    q_dv = q_d.rearrange("(r p) c -> p r c", p=128)

    sems = {}
    done = {}

    with (
        nc.sbuf_tensor("h128_s", [128, 128], BF16) as h128_s,
        nc.sbuf_tensor("h64_s", [64, 64], BF16) as h64_s,
        nc.sbuf_tensor("maskT_s", [128, 128], BF16) as maskT_s,
        nc.sbuf_tensor("ident_s", [128, 128], F32) as ident_s,
        nc.sbuf_tensor("pmask_s", [128, 8], F32) as pmask_s,
        nc.sbuf_tensor("stats", [128, 128], F32) as stats,
        nc.sbuf_tensor("xio", [128, 2 * D], F32) as xio,
        nc.sbuf_tensor("sq_s", [128, 2 * D], F32) as sq_s,
        nc.sbuf_tensor("xnb", [128, 2 * D], BF16) as xnb,
        nc.sbuf_tensor("bufA", [128, 16 * 2048], BF16) as bufA,
        nc.psum_tensor("ps0", [128, 512], F32) as ps0,
        nc.psum_tensor("ps1", [128, 512], F32) as ps1,
        nc.psum_tensor("ps2", [128, 512], F32) as ps2,
        nc.psum_tensor("ps3", [128, 512], F32) as ps3,
        nc.psum_tensor("ps4", [128, 512], F32) as ps4,
        nc.psum_tensor("ps5", [128, 512], F32) as ps5,
        nc.psum_tensor("ps6", [128, 512], F32) as ps6,
        nc.psum_tensor("ps7", [128, 512], F32) as ps7,
        nc.Block() as block,
    ):
        PS = [ps0, ps1, ps2, ps3, ps4, ps5, ps6, ps7]

        def sem(name):
            if name not in sems:
                sems[name] = Sem(nc, name)
            return sems[name]

        s_const = sem("const")

        @block.sync
        def _(sy):
            for dst, src in ((h128_s, h128_in), (h64_s, h64_in), (maskT_s, maskT_in),
                             (ident_s, ident_in), (pmask_s, pmask_in)):
                sy.dma_start(out=dst[:, :], in_=src).then_inc(s_const.h, 16)
        s_const.n = 80
        done["const"] = [(s_const, 80)]

        s_w1c = sem("bp_w1c")
        s_w3c = sem("bp_w3c")
        s_mixld = sem("bp_mixld")
        s_mixmm = sem("bp_mixmm")
        s_mixev = sem("bp_mixev")
        s_mixst = sem("bp_mixst")
        s_tp = sem("bp_tp")
        s_tst = sem("bp_tst")
        DO_BP = "bprep" in phases

        def emit_cast_chunk(g, which, j):
            srcw = w1_in if which == 0 else w3_in
            dstw = w1b if which == 0 else w3b
            sm = s_w1c if which == 0 else s_w3c
            g.dma_start(out=dstw[j * 1024:(j + 1) * 1024, :],
                        in_=srcw[j * 1024:(j + 1) * 1024, :]).then_inc(sm.h, 16)
            sm.n += 16

        def ln_batched(tag, src_dram, dst_dram, pre_wait, gp_hook=None):
            s_ld = sem(tag + "_ld")
            s_r1 = sem(tag + "_r1")
            s_sq = sem(tag + "_sq")
            s_r2 = sem(tag + "_r2")
            s_ch = sem(tag + "_ch")
            s_ld2 = sem(tag + "_ld2")
            s_ap = sem(tag + "_ap")
            s_st = sem(tag + "_st")

            @block.sync
            def _(sy):
                for sm, cn in pre_wait:
                    sy.wait_ge(sm.h, cn)
                for tt in range(TT):
                    if tt >= 2:
                        sy.wait_ge(s_r1.h, tt - 1)
                        sy.wait_ge(s_sq.h, tt - 1)
                    sy.dma_start(out=xio[:, (tt % 2) * D:(tt % 2 + 1) * D],
                                 in_=src_dram[tt * 128:(tt + 1) * 128, :]).then_inc(s_ld.h, 16)
                    s_ld.n += 16
                for tt in range(TT):
                    if tt < 2:
                        sy.wait_ge(s_ch.h, 1)
                    else:
                        sy.wait_ge(s_ap.h, tt - 1)
                    sy.dma_start(out=xio[:, (tt % 2) * D:(tt % 2 + 1) * D],
                                 in_=src_dram[tt * 128:(tt + 1) * 128, :]).then_inc(s_ld2.h, 16)
                    s_ld2.n += 16

            @block.scalar
            def _(sc):
                for tt in range(TT):
                    sc.wait_ge(s_ld.h, (tt + 1) * 16)
                    if tt >= 2:
                        sc.wait_ge(s_r2.h, tt - 1)
                    sc.activation(sq_s[:, (tt % 2) * D:(tt % 2 + 1) * D],
                                  xio[:, (tt % 2) * D:(tt % 2 + 1) * D],
                                  AF.Square).then_inc(s_sq.h, 1)
                    s_sq.n += 1
                sc.wait_ge(s_ch.h, 1)
                sc.activation(stats[:, 48:48 + TT], stats[:, 80:80 + TT], AF.Ln)
                sc.drain()
                sc.activation(stats[:, 96:96 + TT], stats[:, 48:48 + TT], AF.Exp,
                              scale=-0.5).then_inc(s_ch.h, 1)
                s_ch.n += 1
                for tt in range(TT):
                    sc.wait_ge(s_ch.h, 3)
                    sc.wait_ge(s_ld2.h, (tt + 1) * 16)
                    if tt >= 2:
                        sc.wait_ge(s_st.h, (tt - 1) * 16)
                    sc.activation(xnb[:, (tt % 2) * D:(tt % 2 + 1) * D],
                                  xio[:, (tt % 2) * D:(tt % 2 + 1) * D], AF.Identity,
                                  bias=stats[:, 112 + tt:113 + tt],
                                  scale=stats[:, 96 + tt:97 + tt]).then_inc(s_ap.h, 1)
                    s_ap.n += 1

            @block.vector
            def _(ve):
                for tt in range(TT):
                    ve.wait_ge(s_ld.h, (tt + 1) * 16)
                    ve.reduce_sum(stats[:, tt:tt + 1],
                                  xio[:, (tt % 2) * D:(tt % 2 + 1) * D],
                                  axis=AX.X).then_inc(s_r1.h, 1)
                    s_r1.n += 1
                    ve.wait_ge(s_sq.h, tt + 1)
                    ve.reduce_sum(stats[:, 16 + tt:17 + tt],
                                  sq_s[:, (tt % 2) * D:(tt % 2 + 1) * D],
                                  axis=AX.X).then_inc(s_r2.h, 1)
                    s_r2.n += 1
                ve.drain()
                ve.tensor_scalar(stats[:, 32:32 + TT], stats[:, 0:TT], 1.0 / D, None, ALU.mult)
                ve.tensor_scalar(stats[:, 80:80 + TT], stats[:, 16:16 + TT], 1.0 / D, None, ALU.mult)
                ve.drain()
                ve.tensor_tensor(stats[:, 64:64 + TT], stats[:, 32:32 + TT],
                                 stats[:, 32:32 + TT], ALU.mult)
                ve.drain()
                ve.tensor_tensor(stats[:, 80:80 + TT], stats[:, 80:80 + TT],
                                 stats[:, 64:64 + TT], ALU.subtract)
                ve.drain()
                ve.tensor_scalar(stats[:, 80:80 + TT], stats[:, 80:80 + TT],
                                 EPS, None, ALU.add).then_inc(s_ch.h, 1)
                s_ch.n += 1
                ve.wait_ge(s_ch.h, 2)
                ve.scalar_tensor_tensor(stats[:, 112:112 + TT], stats[:, 32:32 + TT],
                                        -1.0, stats[:, 96:96 + TT],
                                        ALU.mult, ALU.mult).then_inc(s_ch.h, 1)
                s_ch.n += 1

            @block.gpsimd
            def _(g):
                for tt in range(TT):
                    g.wait_ge(s_ap.h, tt + 1)
                    g.dma_start(out=dst_dram[tt * 128:(tt + 1) * 128, :],
                                in_=xnb[:, (tt % 2) * D:(tt % 2 + 1) * D]).then_inc(s_st.h, 16)
                    s_st.n += 16
                    if gp_hook is not None:
                        gp_hook(g, tt)
            return [(s_st, TT * 16)]

        def ln_tile_vec(ve, sv, sa, i, src, extra_op):
            ve.reduce_sum(stats[:, 0:1], src, axis=AX.X)
            ve.drain()
            ve.tensor_scalar(stats[:, 2:3], stats[:, 0:1], 1.0 / D, None,
                             ALU.mult).then_inc(sv.h, 1)
            sv.n += 1
            ve.wait_ge(sa.h, 3 * i + 1)
            ve.reduce_sum(stats[:, 1:2], sq_s[:, 0:D], axis=AX.X)
            ve.drain()
            ve.tensor_scalar(stats[:, 3:4], stats[:, 1:2], 1.0 / D, None, ALU.mult)
            ve.drain()
            ve.tensor_tensor(stats[:, 4:5], stats[:, 2:3], stats[:, 2:3], ALU.mult)
            ve.drain()
            ve.tensor_tensor(stats[:, 5:6], stats[:, 3:4], stats[:, 4:5], ALU.subtract)
            ve.drain()
            ve.tensor_scalar(stats[:, 5:6], stats[:, 5:6], EPS, None,
                             ALU.add).then_inc(sv.h, 1)
            sv.n += 1
            ve.wait_ge(sa.h, 3 * i + 2)
            ve.scalar_tensor_tensor(stats[:, 7:8], stats[:, 2:3], -1.0,
                                    stats[:, 6:7], ALU.mult, ALU.mult).then_inc(sv.h, 1)
            sv.n += 1
            ve.wait_ge(sa.h, 3 * i + 3)
            extra_op(ve)
            sv.n += 1

        def ln_tile_act(sc, sv, sa, i, src, apply_dst):
            sc.activation(sq_s[:, 0:D], src, AF.Square).then_inc(sa.h, 1)
            sa.n += 1
            sc.wait_ge(sv.h, 4 * i + 2)
            sc.activation(stats[:, 6:7], stats[:, 5:6], AF.Ln)
            sc.drain()
            sc.activation(stats[:, 6:7], stats[:, 6:7], AF.Exp,
                          scale=-0.5).then_inc(sa.h, 1)
            sa.n += 1
            sc.wait_ge(sv.h, 4 * i + 3)
            sc.activation(apply_dst, src, AF.Identity, bias=stats[:, 7:8],
                          scale=stats[:, 6:7]).then_inc(sa.h, 1)
            sa.n += 1

        if "a1a" in phases:
            def a1a_hook(g, tt):
                if DO_BP and tt < 8:
                    emit_cast_chunk(g, 0, tt)
            done["a1a"] = ln_batched("ln1", x_in, xn_d, [],
                                     gp_hook=a1a_hook if DO_BP else None)

        if "a1b" in phases:
            if True:
                st_a1b = ExitStack()
                s_sb = st_a1b.enter_context(nc.sbuf_tensor("s_sb", [128, 8 * 128], F32))
                wsl = st_a1b.enter_context(nc.sbuf_tensor("wsl", [128, 2 * 16 * 512], BF16))
                kvu = st_a1b.enter_context(nc.sbuf_tensor("kvu", [128, 3 * TT * 512], BF16))
                sig2 = st_a1b.enter_context(nc.sbuf_tensor("sig2", [128, 2 * 512], F32))
                VOFF, KOFF, UOFF = 0, TT * 512, 2 * TT * 512
                s_xnfm = sem("a1b_xnfm")
                s_wld = sem("a1b_wld")
                s_pmm = sem("a1b_pmm")
                s_pev = sem("a1b_pev")
                s_uev = sem("a1b_uev")
                s_smm = sem("a1b_smm")
                s_sev = sem("a1b_sev")
                s_spill = sem("a1b_spill")
                s_slst = sem("a1b_slst")

                @block.sync
                def _(sy):
                    sy.wait_ge(sems["ln1_st"].h, TT * 16)
                    for dsl in range(16):
                        sy.dma_start(out=bufA[:, dsl * T:(dsl + 1) * T],
                                     in_=xn_d[:, dsl * 128:(dsl + 1) * 128],
                                     transpose=True).then_inc(s_xnfm.h, 16)
                    s_xnfm.n = 256

                W3L = [wv_in, wb_in, wk_in]

                def emit_sstore(g, ec):
                    g.wait_ge(s_sev.h, (ec + 1) * 2)
                    for b in range(2):
                        dst = s_loc.ap().rearrange("(bh e) d -> e bh d", e=128)[
                            :, b * 16 + ec * 4: b * 16 + ec * 4 + 4, :]
                        g.dma_start(out=dst,
                                    in_=s_sb[:, b * 512:(b + 1) * 512].rearrange(
                                        "p (hl d) -> p hl d", d=128)).then_inc(s_slst.h, 16)
                        s_slst.n += 16

                def emit_spill(g, ec):
                    g.wait_ge(s_uev.h, (ec + 1) * TT)
                    g.dma_start(out=u_dv[:, :, ec * 512:(ec + 1) * 512],
                                in_=kvu[:, UOFF:UOFF + TT * 512].rearrange(
                                    "p (ts e) -> p ts e", e=512)).then_inc(s_spill.h, 16)
                    g.wait_ge(s_pev.h, (ec * 3 + 2) * TT + TT)
                    g.dma_start(out=k_dv[:, :, ec * 512:(ec + 1) * 512],
                                in_=kvu[:, KOFF:KOFF + TT * 512].rearrange(
                                    "p (ts e) -> p ts e", e=512)).then_inc(s_spill.h, 16)
                    s_spill.n += 32

                @block.gpsimd
                def _(g):
                    g.wait_ge(s_xnfm.h, 256)
                    idx = 0
                    for ec in range(4):
                        if ec > 0:
                            emit_spill(g, ec - 1)
                            emit_sstore(g, ec - 1)
                        for wi in range(3):
                            if idx >= 2:
                                g.wait_ge(s_pmm.h, (idx - 1) * TT)
                            src = W3L[wi].rearrange("(dt p) e -> p dt e", p=128)[
                                :, :, ec * 512:(ec + 1) * 512]
                            g.dma_start(
                                out=wsl[:, (idx % 2) * 8192:(idx % 2 + 1) * 8192].rearrange(
                                    "p (dt e) -> p dt e", e=512),
                                in_=src).then_inc(s_wld.h, 16)
                            s_wld.n += 16
                            idx += 1
                            if DO_BP and idx <= 8:
                                emit_cast_chunk(g, 1, idx - 1)
                    emit_spill(g, 3)
                    emit_sstore(g, 3)

                @block.tensor
                def _(t):
                    idx = 0
                    for ec in range(4):
                        for wi in range(3):
                            t.wait_ge(s_wld.h, (idx + 1) * 16)
                            for ts in range(TT):
                                gi = idx * TT + ts
                                pb = PS[gi % 2]
                                if gi >= 2:
                                    t.wait_ge(s_pev.h, gi - 1)
                                for dt in range(16):
                                    mm = t.matmul(
                                        pb[:, :],
                                        bufA[:, dt * T + ts * 128: dt * T + (ts + 1) * 128],
                                        wsl[:, (idx % 2) * 8192 + dt * 512:
                                            (idx % 2) * 8192 + (dt + 1) * 512],
                                        start=(dt == 0), stop=(dt == 15))
                                mm.then_inc(s_pmm.h, 1)
                                s_pmm.n += 1
                            idx += 1
                        t.wait_ge(s_pev.h, (idx - 1) * TT + TT)
                        t.wait_ge(s_uev.h, (ec + 1) * TT)
                        if ec >= 1:
                            t.wait_ge(s_sev.h, ec * 2)
                        for hl in range(4):
                            for ch in range(NCH):
                                pb = PS[2] if ch < NCHB else PS[3]
                                mm = t.matmul(
                                    pb[:, hl * 128:(hl + 1) * 128],
                                    kvu[:, KOFF + ch * 512 + hl * 128:
                                        KOFF + ch * 512 + (hl + 1) * 128],
                                    kvu[:, UOFF + ch * 512 + hl * 128:
                                        UOFF + ch * 512 + (hl + 1) * 128],
                                    start=(ch % NCHB == 0), stop=(ch % NCHB == NCHB - 1))
                        mm.then_inc(s_smm.h, 1)
                        s_smm.n += 1

                @block.scalar
                def _(sc):
                    idx = 0
                    for ec in range(4):
                        for wi in range(3):
                            for ts in range(TT):
                                gi = idx * TT + ts
                                pb = PS[gi % 2]
                                sc.wait_ge(s_pmm.h, gi + 1)
                                if ec >= 1 and wi == 0 and ts == 0:
                                    sc.wait_ge(s_smm.h, ec)
                                    sc.wait_ge(s_spill.h, ec * 32)
                                if wi == 0:
                                    sc.activation(kvu[:, VOFF + ts * 512:VOFF + (ts + 1) * 512],
                                                  pb[:, :], AF.Copy).then_inc(s_pev.h, 1)
                                elif wi == 1:
                                    bi = ec * TT + ts
                                    if bi >= 2:
                                        sc.wait_ge(s_uev.h, bi - 1)
                                    sc.activation(sig2[:, (bi % 2) * 512:(bi % 2 + 1) * 512],
                                                  pb[:, :], AF.Sigmoid).then_inc(s_pev.h, 1)
                                else:
                                    sc.activation(kvu[:, KOFF + ts * 512:KOFF + (ts + 1) * 512],
                                                  pb[:, :], AF.Copy).then_inc(s_pev.h, 1)
                                s_pev.n += 1
                            idx += 1
                        sc.wait_ge(s_smm.h, ec + 1)
                        if ec >= 1:
                            sc.wait_ge(s_slst.h, ec * 32)
                        for b in range(2):
                            pb = PS[2 + b]
                            for hl in range(4):
                                ins = sc.activation(s_sb[:, (b * 4 + hl) * 128:(b * 4 + hl + 1) * 128],
                                                    pb[:, hl * 128:(hl + 1) * 128], AF.Copy)
                            ins.then_inc(s_sev.h, 1)
                            s_sev.n += 1

                @block.vector
                def _(ve):
                    for ec in range(4):
                        for ts in range(TT):
                            bi = ec * TT + ts
                            ve.wait_ge(s_pev.h, (ec * 3 + 1) * TT + ts + 1)
                            ve.tensor_tensor(kvu[:, UOFF + ts * 512:UOFF + (ts + 1) * 512],
                                             sig2[:, (bi % 2) * 512:(bi % 2 + 1) * 512],
                                             kvu[:, VOFF + ts * 512:VOFF + (ts + 1) * 512],
                                             ALU.mult).then_inc(s_uev.h, 1)
                            s_uev.n += 1

                done["a1b"] = [(s_spill, 128), (s_slst, 128), (s_sev, 8)]

                if "a2" in phases:
                    s_cc = sem("a2_cc")
                    s_pref = sem("a2_pref")

                    @block.gpsimd
                    def _(g):
                        g.wait_ge(s_spill.h, 128)
                        g.wait_ge(sems["a1b_slst"].h, 128)
                        g.collective_compute(
                            "AllGather", ALU.bypass,
                            replica_groups=[list(range(NCORES))],
                            ins=[s_loc.ap().opt()], outs=[s_all.ap().opt()],
                        ).then_inc(s_cc.h, 1)
                        s_cc.n = 1
                    done["a2cc"] = [(s_cc, 1)]

        if "a1b" in phases:
            st_a1b.close()

        if "a2" in phases:
            if True:
                st_pref = ExitStack()
                tbuf = st_pref.enter_context(nc.sbuf_tensor("tbuf", [128, 2 * 2048], BF16))
                st_mid = ExitStack()
                prefix = st_mid.enter_context(nc.sbuf_tensor("prefix", [128, 32 * 128], F32))
                mix_s = st_mid.enter_context(nc.sbuf_tensor("mix_s", [128, 4096], BF16))
                s_cc = sems["a2_cc"]
                s_pref = sems["a2_pref"]

                tp_i = [0]

                def emit_tp(sy, entries):
                    for (tag, srcb, dstb, sm, smcnt, sl, rc) in entries:
                        i = tp_i[0]
                        if smcnt:
                            sy.wait_ge(sm.h, smcnt)
                        if i >= 2:
                            sy.wait_ge(s_tst.h, (i - 1) * 16)
                        sy.dma_start(out=tbuf[:, (i % 2) * 2048:(i % 2 + 1) * 2048],
                                     in_=srcb[rc * 2048:(rc + 1) * 2048,
                                              sl * 128:(sl + 1) * 128],
                                     transpose=True).then_inc(s_tp.h, 16)
                        s_tp.n += 16
                        sy.wait_ge(s_tp.h, (i + 1) * 16)
                        sy.dma_start(out=dstb[sl * 128:(sl + 1) * 128,
                                              rc * 2048:(rc + 1) * 2048],
                                     in_=tbuf[:, (i % 2) * 2048:(i % 2 + 1) * 2048]
                                     ).then_inc(s_tst.h, 16)
                        s_tst.n += 16
                        tp_i[0] += 1

                if DO_BP:
                    TP1 = ([("w1", w1b, w1tb, s_w1c, 128 if (dsl, rc) == (0, 0) else 0,
                             dsl, rc) for dsl in range(16) for rc in range(4)] +
                           [("w3", w3b, w3tb, s_w3c, 128 if (dsl, rc) == (0, 0) else 0,
                             dsl, rc) for dsl in range(16) for rc in range(4)])
                    TP2 = [("w2h", w2hb, w2htb, s_mixst, 1024 if hs == 0 else 0, hs, 0)
                           for hs in range(64)]

                    @block.sync
                    def _(sy):
                        sy.wait_ge(sems["a1b_spill"].h, 128)
                        sy.wait_ge(sems["a1b_slst"].h, 128)
                        emit_tp(sy, TP1)
                    done["bprep"] = [(s_tst, 192 * 16)]
                st_g = ExitStack()
                sgath = st_g.enter_context(nc.sbuf_tensor("sgath", [128, 2 * 32 * 128], F32))
                if True:
                    s_gld = sem("a2_gld")

                    @block.sync
                    def _(sy):
                        sy.wait_ge(s_cc.h, 1)
                        for c in range(NCORES):
                            if c >= 2:
                                sy.wait_ge(s_pref.h, c - 1)
                            srcg = s_all.ap()[c * 4096:(c + 1) * 4096, :].rearrange(
                                "(bh e) d -> e bh d", e=128)
                            sy.dma_start(out=sgath[:, (c % 2) * 4096:(c % 2 + 1) * 4096]
                                         .rearrange("p (bh d) -> p bh d", d=128),
                                         in_=srcg).then_inc(s_gld.h, 16)
                            s_gld.n += 16

                    @block.gpsimd
                    def _(g):
                        if DO_BP:
                            for j in range(64):
                                if j >= 1:
                                    g.wait_ge(s_mixmm.h, j * 8)
                                srcm = w2_in.rearrange("d (a b) -> a d b", b=128)[
                                    :, j * 32:(j + 1) * 32, :]
                                g.dma_start(out=mix_s[0:64, 0:4096].rearrange(
                                    "p (dd b) -> p dd b", b=128),
                                    in_=srcm).then_inc(s_mixld.h, 16)
                                s_mixld.n += 16
                                if j >= 1:
                                    jp = j - 1
                                    g.wait_ge(s_mixev.h, jp * 8 + 8)
                                    dstm = w2hb.rearrange("d (a b) -> a d b", b=128)[
                                        :, jp * 32:(jp + 1) * 32, :]
                                    g.dma_start(out=dstm,
                                                in_=mix_s[64:128, 0:4096].rearrange(
                                                    "p (dd b) -> p dd b", b=128)
                                                ).then_inc(s_mixst.h, 16)
                                    s_mixst.n += 16
                            g.wait_ge(s_mixev.h, 64 * 8)
                            dstm = w2hb.rearrange("d (a b) -> a d b", b=128)[
                                :, 63 * 32:64 * 32, :]
                            g.dma_start(out=dstm, in_=mix_s[64:128, 0:4096].rearrange(
                                "p (dd b) -> p dd b", b=128)).then_inc(s_mixst.h, 16)
                            s_mixst.n += 16

                    if DO_BP:
                        @block.tensor
                        def _(t):
                            t.wait_ge(s_const.h, 80)
                            for gi in range(512):
                                j, jj = gi // 8, gi % 8
                                t.wait_ge(s_mixld.h, (j + 1) * 16)
                                if gi >= 4:
                                    t.wait_ge(s_mixev.h, gi - 3)
                                pbm = PS[4 + gi % 4]
                                mm = t.matmul(pbm[0:64, :], h64_s[:, :],
                                              mix_s[0:64, jj * 512:(jj + 1) * 512],
                                              start=True, stop=True)
                                mm.then_inc(s_mixmm.h, 1)
                                s_mixmm.n += 1

                        @block.vector
                        def _(ve):
                            ve.wait_ge(sems["a1b_spill"].h, 128)
                            for gi in range(512):
                                j, jj = gi // 8, gi % 8
                                ve.wait_ge(s_mixmm.h, gi + 1)
                                if jj == 0 and j >= 1:
                                    ve.wait_ge(s_mixst.h, j * 16)
                                pbm = PS[4 + gi % 4]
                                ve.tensor_copy(mix_s[64:128, jj * 512:(jj + 1) * 512],
                                               pbm[0:64, :]).then_inc(s_mixev.h, 1)
                                s_mixev.n += 1

                    @block.vector
                    def _(ve):
                        ve.wait_ge(sems["a1b_slst"].h, 128)
                        ve.wait_ge(s_mixst.h, 64 * 16)
                        ve.memset(prefix[:, :], 0.0)
                        ve.drain()
                        for c in range(NCORES):
                            ve.wait_ge(s_gld.h, (c + 1) * 16)
                            ve.scalar_tensor_tensor(
                                prefix[:, :], sgath[:, (c % 2) * 4096:(c % 2 + 1) * 4096],
                                pmask_s[:, c:c + 1], prefix[:, :],
                                ALU.mult, ALU.add)
                            ve.drain()
                            ve.nop().then_inc(s_pref.h, 1)
                            s_pref.n += 1
                    done["a2"] = [(s_pref, NCORES)]

                if "a3a" in phases:
                    st_g.close()
                    st_a3a = ExitStack()
                    qsl = st_a3a.enter_context(nc.sbuf_tensor("qsl", [128, 4 * 512], BF16))
                    wqsl = st_a3a.enter_context(nc.sbuf_tensor("wqsl", [128, 2 * 16 * 128], BF16))
                    if True:
                        s_qwld = sem("a3_qwld")
                        s_qmm = sem("a3_qmm")
                        s_qev = sem("a3_qev")
                        s_qst = sem("a3_qst")

                        @block.gpsimd
                        def _(g):
                            g.wait_ge(sems["a2_pref"].h, NCORES)
                            for et in range(16):
                                if et >= 2:
                                    g.wait_ge(s_qmm.h, (et - 1) * NTC)
                                src = wq_in.rearrange("(dt p) e -> p dt e", p=128)[
                                    :, :, et * 128:(et + 1) * 128]
                                g.dma_start(out=wqsl[:, (et % 2) * 2048:(et % 2 + 1) * 2048]
                                            .rearrange("p (dt e) -> p dt e", e=128),
                                            in_=src).then_inc(s_qwld.h, 16)
                                s_qwld.n += 16

                        @block.tensor
                        def _(t):
                            for et in range(16):
                                t.wait_ge(s_qwld.h, (et + 1) * 16)
                                for tc in range(NTC):
                                    gi = et * NTC + tc
                                    pb = PS[gi % 2]
                                    if gi >= 2:
                                        t.wait_ge(s_qev.h, gi - 1)
                                    for dt in range(16):
                                        mm = t.matmul(
                                            pb[:, :],
                                            wqsl[:, (et % 2) * 2048 + dt * 128:
                                                 (et % 2) * 2048 + (dt + 1) * 128],
                                            bufA[:, dt * T + tc * 512: dt * T + (tc + 1) * 512],
                                            start=(dt == 0), stop=(dt == 15))
                                    mm.then_inc(s_qmm.h, 1)
                                    s_qmm.n += 1

                        @block.scalar
                        def _(sc):
                            for et in range(16):
                                for tc in range(NTC):
                                    gi = et * NTC + tc
                                    sc.wait_ge(s_qmm.h, gi + 1)
                                    if gi >= 4:
                                        sc.wait_ge(s_qst.h, (gi - 3) * 16)
                                    sc.activation(
                                        qsl[:, (gi % 4) * 512:(gi % 4 + 1) * 512],
                                        PS[gi % 2][:, :], AF.Copy).then_inc(s_qev.h, 1)
                                    s_qev.n += 1

                        @block.sync
                        def _(sy):
                            for et in range(16):
                                for tc in range(NTC):
                                    gi = et * NTC + tc
                                    sy.wait_ge(s_qev.h, gi + 1)
                                    sy.dma_start(out=q_d[et * 128:(et + 1) * 128,
                                                         tc * 512:(tc + 1) * 512],
                                                 in_=qsl[:, (gi % 4) * 512:(gi % 4 + 1) * 512]
                                                 ).then_inc(s_qst.h, 16)
                                    s_qst.n += 16
                        done["a3a"] = [(s_qst, 16 * NTC * 16)]

                if "a3b" in phases:
                    st_a3a.close()
                    st_a3b = ExitStack()
                    kfm = st_a3b.enter_context(nc.sbuf_tensor("kfm", [128, 16 * TB], BF16))
                    chbuf = st_a3b.enter_context(nc.sbuf_tensor("chbuf", [128, 2 * 3 * 2048], BF16))
                    state_bf = st_a3b.enter_context(nc.sbuf_tensor("state_bf", [128, 16 * 128], BF16))
                    att_sb = st_a3b.enter_context(nc.sbuf_tensor("att_sb", [128, 16 * 128], BF16))
                    if True:
                        s_kfm = sem("a3_kfm")
                        s_chld = sem("a3_chld")
                        s_att = sem("a3_att")
                        s_mask = sem("a3_mask")
                        s_omm = sem("a3_omm")
                        s_oev = sem("a3_oev")
                        s_inc = sem("a3_inc")
                        s_sevac = sem("a3_sevac")
                        s_qst = sems["a3_qst"]
                        s_spill = sems["a1b_spill"]
                        s_pref = sems["a2_pref"]
                        CH3 = 3 * 2048

                        inc_cnt = {}
                        omm_cnt = {}

                        @block.sync
                        def _(sy):
                            sy.wait_ge(s_spill.h, 128)
                            sy.wait_ge(s_qst.h, 16 * NTC * 16)
                            for half in range(2):
                                if half == 1:
                                    sy.wait_ge(s_att.h, NCHB * 4)
                                for et in range(16):
                                    sy.dma_start(out=kfm[:, et * TB:(et + 1) * TB],
                                                 in_=k_d[half * TB:(half + 1) * TB,
                                                         et * 128:(et + 1) * 128],
                                                 transpose=True).then_inc(s_kfm.h, 16)
                                    s_kfm.n += 16
                            if DO_BP:
                                emit_tp(sy, TP2)

                        @block.scalar
                        def _(sc):
                            sc.wait_ge(s_spill.h, 128)
                            sc.wait_ge(s_qst.h, 16 * NTC * 16)
                            for ch in range(2):
                                off = (ch % 2) * CH3
                                sc.dma_start(out=chbuf[:, off:off + 2048],
                                             in_=k_d[ch * 128:(ch + 1) * 128, :]).then_inc(s_chld.h, 16)
                                sc.dma_start(out=chbuf[:, off + 2048:off + 4096],
                                             in_=u_d[ch * 128:(ch + 1) * 128, :]).then_inc(s_chld.h, 16)
                                sc.dma_start(out=chbuf[:, off + 4096:off + 6144]
                                             .rearrange("p (et t) -> p et t", t=128),
                                             in_=q_dv[:, :, ch * 128:(ch + 1) * 128]).then_inc(s_chld.h, 16)
                                s_chld.n += 48

                        if "scanprobe" in dbg:
                            s_sprobe = sem("a3_sprobe")

                            @block.gpsimd
                            def _(g):
                                g.wait_ge(s_oev.h, NH)
                                g.dma_start(out=sp_dbg[:, 0:2048], in_=att_sb[:, :]).then_inc(s_sprobe.h, 16)
                                g.dma_start(out=sp_dbg[:, 2048:8192], in_=chbuf[:, 0:6144]).then_inc(s_sprobe.h, 16)
                                g.dma_start(out=sp_dbg[:, 8192:10240], in_=state_bf[:, :]).then_inc(s_sprobe.h, 16)
                                g.dma_start(out=sp_dbg[:, 10240:10240 + 16 * TB], in_=kfm[:, :]).then_inc(s_sprobe.h, 16)
                                s_sprobe.n = 64

                        @block.tensor
                        def _(t):
                            t.wait_ge(s_pref.h, NCORES)
                            t.wait_ge(s_const.h, 80)
                            for ch in range(NCH):
                                if ch == 1 and "scanprobe" in dbg:
                                    t.wait_ge(sems["a3_sprobe"].h, 64)
                                half = 0 if ch < NCHB else 1
                                first = (ch % NCHB == 0)
                                t.wait_ge(s_chld.h, (ch + 1) * 48)
                                t.wait_ge(s_kfm.h, (half + 1) * 256)
                                off = (ch % 2) * CH3
                                tloc = (ch % NCHB) * 128
                                if first:
                                    if half == 1:
                                        t.wait_ge(s_sevac.h, NCHB * NH)
                                    for h in range(NH):
                                        bh = half * 16 + h
                                        pb = PS[4 + h // 4]
                                        mm = t.matmul(pb[:, (h % 4) * 128:(h % 4 + 1) * 128],
                                                      ident_s[:, :],
                                                      prefix[:, bh * 128:(bh + 1) * 128],
                                                      start=(h % 4 == 0), stop=False,
                                                      skip_group_check=True)
                                    mm.then_inc(s_inc.h, 1)
                                    s_inc.n += 1
                                for g4 in range(4):
                                    AG = ch * 4 + g4
                                    pb_att = PS[g4 % 2]
                                    if AG >= 2:
                                        t.wait_ge(s_mask.h, (AG - 1) * 4)
                                    for hh in range(4):
                                        h = g4 * 4 + hh
                                        mm = t.matmul(pb_att[:, hh * 128:(hh + 1) * 128],
                                                      kfm[:, h * TB + tloc: h * TB + tloc + 128],
                                                      chbuf[:, off + 4096 + h * 128:
                                                            off + 4096 + (h + 1) * 128],
                                                      start=True, stop=True)
                                    mm.then_inc(s_att.h, 1)
                                    s_att.n += 1
                                for h in range(NH):
                                    gidx = ch * NH + h
                                    g4 = h // 4
                                    OG = ch * 4 + g4
                                    pb_o = PS[2 + g4 % 2]
                                    oslot = h % 8
                                    if h % 4 == 0 and OG >= 2:
                                        t.wait_ge(s_oev.h, (OG - 1) * 4)
                                    t.wait_ge(s_mask.h, gidx + 1)
                                    t.wait_ge(s_sevac.h, gidx + 1)
                                    t.matmul(pb_o[:, (h % 4) * 128:(h % 4 + 1) * 128],
                                             chbuf[:, off + 2048 + h * 128:
                                                   off + 2048 + (h + 1) * 128],
                                             att_sb[:, h * 128:(h + 1) * 128],
                                             start=True, stop=False)
                                    mm = t.matmul(pb_o[:, (h % 4) * 128:(h % 4 + 1) * 128],
                                                  state_bf[:, h * 128:(h + 1) * 128],
                                                  chbuf[:, off + 4096 + h * 128:
                                                        off + 4096 + (h + 1) * 128],
                                                  start=False, stop=True)
                                    mm.then_inc(s_omm.h, 1)
                                    s_omm.n += 1
                                    omm_cnt[(ch, h)] = s_omm.n
                                    pb_s = PS[4 + h // 4]
                                    t.wait_ge(s_sevac.h, ch * NH + (h // 4 + 1) * 4)
                                    mm = t.matmul(pb_s[:, (h % 4) * 128:(h % 4 + 1) * 128],
                                                  chbuf[:, off + h * 128: off + (h + 1) * 128],
                                                  chbuf[:, off + 2048 + h * 128:
                                                        off + 2048 + (h + 1) * 128],
                                                  start=False,
                                                  stop=(ch % NCHB == NCHB - 1 and h % 4 == 3),
                                                  skip_group_check=True)
                                    mm.then_inc(s_inc.h, 1)
                                    s_inc.n += 1
                                    inc_cnt[(ch, h)] = s_inc.n

                        @block.vector
                        def _(ve):
                            for ch in range(NCH):
                                if ch == 1 and "scanprobe" in dbg:
                                    ve.wait_ge(sems["a3_sprobe"].h, 64)
                                for h in range(NH):
                                    gidx = ch * NH + h
                                    g4 = h // 4
                                    AG = ch * 4 + g4
                                    pb_att = PS[g4 % 2]
                                    ve.wait_ge(s_att.h, AG + 1)
                                    if ch >= 1:
                                        ve.wait_ge(s_omm.h, (ch - 1) * NH + h + 1)
                                    ve.tensor_tensor(
                                        att_sb[:, h * 128:(h + 1) * 128],
                                        pb_att[:, (h % 4) * 128:(h % 4 + 1) * 128],
                                        maskT_s[:, :], ALU.mult).then_inc(s_mask.h, 1)
                                    s_mask.n += 1

                        @block.scalar
                        def _(sc):
                            for ch in range(NCH):
                                if ch == 1 and "scanprobe" in dbg:
                                    sc.wait_ge(sems["a3_sprobe"].h, 64)
                                half = 0 if ch < NCHB else 1
                                first = (ch % NCHB == 0)
                                for h in range(NH):
                                    pb_s = PS[4 + h // 4]
                                    if first:
                                        sc.wait_ge(s_inc.h, 1 if half == 0
                                                   else NCHB * NH + 2)
                                        if half == 1:
                                            sc.wait_ge(s_omm.h, omm_cnt[(NCHB - 1, h)])
                                    else:
                                        sc.wait_ge(s_inc.h, inc_cnt[(ch - 1, (h // 4) * 4 + 3)])
                                    sc.activation(state_bf[:, h * 128:(h + 1) * 128],
                                                  pb_s[:, (h % 4) * 128:(h % 4 + 1) * 128],
                                                  AF.Copy).then_inc(s_sevac.h, 1)
                                    s_sevac.n += 1
                                for h in range(NH):
                                    gidx = ch * NH + h
                                    g4 = h // 4
                                    pb_o = PS[2 + g4 % 2]
                                    sc.wait_ge(s_omm.h, omm_cnt[(ch, g4 * 4 + 3)])
                                    sc.activation(bufA[:, h * T + ch * 128: h * T + (ch + 1) * 128],
                                                  pb_o[:, (h % 4) * 128:(h % 4 + 1) * 128],
                                                  AF.Copy).then_inc(s_oev.h, 1)
                                    s_oev.n += 1
                                if ch + 2 < NCH:
                                    ch2 = ch + 2
                                    sc.wait_ge(s_inc.h, inc_cnt[(ch, NH - 1)])
                                    off2 = (ch2 % 2) * CH3
                                    sc.dma_start(out=chbuf[:, off2:off2 + 2048],
                                                 in_=k_d[ch2 * 128:(ch2 + 1) * 128, :]).then_inc(s_chld.h, 16)
                                    sc.dma_start(out=chbuf[:, off2 + 2048:off2 + 4096],
                                                 in_=u_d[ch2 * 128:(ch2 + 1) * 128, :]).then_inc(s_chld.h, 16)
                                    sc.dma_start(out=chbuf[:, off2 + 4096:off2 + 6144]
                                                 .rearrange("p (et t) -> p et t", t=128),
                                                 in_=q_dv[:, :, ch2 * 128:(ch2 + 1) * 128]).then_inc(s_chld.h, 16)
                                    s_chld.n += 48
                        done["a3b"] = [(s_oev, NCH * NH)]

                if "a3c" in phases:
                    st_a3b.close()
                    st_mid.close()
                    st_a3c = ExitStack()
                    wosl = st_a3c.enter_context(nc.sbuf_tensor("wosl", [128, 16 * 2048], BF16))
                    ln_out = st_a3c.enter_context(nc.sbuf_tensor("ln_out", [128, 3 * D], F32))
                    if True:
                        s_wold = sem("a3c_wold")
                        s_amm = sem("a3c_amm")
                        s_aev = sem("a3c_aev")
                        s_xld = sem("a3c_xld")
                        sv2 = sem("a3c_sv")
                        sa2 = sem("a3c_sa")
                        s_x2st = sem("a3c_x2st")

                        @block.gpsimd
                        def _(g):
                            g.wait_ge(sems["a3_qmm"].h, 16 * NTC)
                            g.wait_ge(sems["a3_inc"].h, NCH * NH + 2)
                            g.wait_ge(sems["a3_sevac"].h, NCH * NH)
                            if "o_fm" in dbg:
                                s_odbg = sem("a3c_odbg")
                                g.wait_ge(sems["a3_oev"].h, NCH * NH)
                                g.dma_start(out=o_dbg, in_=bufA[:, :16 * T]).then_inc(s_odbg.h, 16)
                                s_odbg.n = 16
                                g.wait_ge(s_odbg.h, 16)
                            g.dma_start(out=wosl[:, :].rearrange("p (et e) -> p et e", e=2048),
                                        in_=wo_in.rearrange("(et p) e -> p et e", p=128)
                                        ).then_inc(s_wold.h, 16)
                            s_wold.n = 16
                            for ts in range(TT):
                                if ts >= 2:
                                    g.wait_ge(sv2.h, (ts - 1) * 4)
                                g.dma_start(out=xio[:, (ts % 2) * D:(ts % 2 + 1) * D],
                                            in_=x_in[ts * 128:(ts + 1) * 128, :]).then_inc(s_xld.h, 16)
                                s_xld.n += 16

                        @block.tensor
                        def _(t):
                            t.wait_ge(s_wold.h, 16)
                            t.wait_ge(sems["a3_oev"].h, NCH * NH)
                            for ts in range(TT):
                                for dc in range(4):
                                    gi = ts * 4 + dc
                                    pb = PS[gi % 2]
                                    if gi >= 2:
                                        t.wait_ge(s_aev.h, gi - 1)
                                    for h in range(NH):
                                        mm = t.matmul(
                                            pb[:, :],
                                            bufA[:, h * T + ts * 128: h * T + (ts + 1) * 128],
                                            wosl[:, h * 2048 + dc * 512: h * 2048 + (dc + 1) * 512],
                                            start=(h == 0), stop=(h == NH - 1))
                                    mm.then_inc(s_amm.h, 1)
                                    s_amm.n += 1

                        @block.vector
                        def _(ve):
                            for ts in range(TT):
                                for dc in range(4):
                                    gi = ts * 4 + dc
                                    ve.wait_ge(s_amm.h, gi + 1)
                                    ins = ve.tensor_copy(ln_out[:, dc * 512:(dc + 1) * 512],
                                                         PS[gi % 2][:, :])
                                    ins.then_inc(s_aev.h, 1)
                                    s_aev.n += 1
                                ve.drain()

                                def extra(ve, ts=ts):
                                    ve.wait_ge(s_xld.h, (ts + 1) * 16)
                                    if ts >= 1:
                                        ve.wait_ge(s_x2st.h, ts * 16)
                                    ve.tensor_tensor(ln_out[:, 2 * D:3 * D],
                                                     ln_out[:, D:2 * D],
                                                     xio[:, (ts % 2) * D:(ts % 2 + 1) * D],
                                                     ALU.add).then_inc(sv2.h, 1)
                                ln_tile_vec(ve, sv2, sa2, ts, ln_out[:, 0:D], extra)

                        @block.scalar
                        def _(sc):
                            for ts in range(TT):
                                sc.wait_ge(s_aev.h, ts * 4 + 4)
                                ln_tile_act(sc, sv2, sa2, ts, ln_out[:, 0:D],
                                            ln_out[:, D:2 * D])

                        @block.sync
                        def _(sy):
                            for ts in range(TT):
                                sy.wait_ge(sv2.h, (ts + 1) * 4)
                                sy.dma_start(out=x2_d[ts * 128:(ts + 1) * 128, :],
                                             in_=ln_out[:, 2 * D:3 * D]).then_inc(s_x2st.h, 16)
                                s_x2st.n += 16
                        done["a3c"] = [(s_x2st, TT * 16)]

        if "a2" in phases:
            if "a3c" in phases:
                st_a3c.close()
            elif "a3b" in phases:
                st_a3b.close()
                st_mid.close()
            elif "a3a" in phases:
                st_a3a.close()
                st_mid.close()
            else:
                st_g.close()
                st_mid.close()
            st_pref.close()

        if "b" in phases:
            done["b_ln3"] = ln_batched("ln3", x2_d, xn2_d,
                                       [(sems["a3c_x2st"], TT * 16)] if "a3c" in phases else [])
            NTCB = min(NTC, int(os.environ.get("KBLIM", "99")))
            st_b = ExitStack()
            xn2fm = st_b.enter_context(nc.sbuf_tensor("xn2fm", [128, 16 * 512], BF16))
            w13sl = st_b.enter_context(nc.sbuf_tensor("w13sl", [128, 4 * 2048], BF16))
            w2sl = st_b.enter_context(nc.sbuf_tensor("w2sl", [128, 2 * 8 * 1024], BF16))
            gbuf = st_b.enter_context(nc.sbuf_tensor("gbuf", [128, 2 * 512], BF16))
            silu_s = st_b.enter_context(nc.sbuf_tensor("silu_s", [128, 2 * 512], F32))
            ffn_sb = st_b.enter_context(nc.sbuf_tensor("ffn_sb", [128, 4 * 2048], BF16))
            lnb_out = st_b.enter_context(nc.sbuf_tensor("lnb_out", [128, 2 * D], F32))
            if True:
                s_xn2 = sem("b_xn2")
                s_w13 = sem("b_w13")
                s_gmm = sem("b_gmm")
                s_gate = sem("b_gate")
                s_silu = sem("b_silu")
                s_gtt = sem("b_gtt")
                s_ymm = sem("b_ymm")
                s_yev = sem("b_yev")
                s_w2ld = sem("b_w2ld")
                s_dmm = sem("b_dmm")
                s_dev = sem("b_dev")
                s_x2ld = sem("b_x2ld")
                svb = sem("b_sv")
                sab = sem("b_sa")
                s_out = sem("b_out")
                s_gprog = sem("b_gprog")

                ln3_st = sems["ln3_st"]

                @block.sync
                def _(sy):
                    sy.wait_ge(ln3_st.h, TT * 16)
                    if "bprep" in phases:
                        sy.wait_ge(sems["bp_tst"].h, 192 * 16)

                    def emit_stores(tc):
                        for ts4 in range(4):
                            i = tc * 4 + ts4
                            sy.wait_ge(svb.h, (i + 1) * 4)
                            sy.dma_start(out=out[i * 128:(i + 1) * 128, :],
                                         in_=lnb_out[:, D:2 * D]).then_inc(s_out.h, 16)
                            s_out.n += 16
                    for tc in range(NTCB):
                        if tc >= 1:
                            sy.wait_ge(s_gmm.h, tc * 64)
                        for dsl in range(16):
                            sy.dma_start(out=xn2fm[:, dsl * 512:(dsl + 1) * 512],
                                         in_=xn2_d[tc * 512:(tc + 1) * 512,
                                                   dsl * 128:(dsl + 1) * 128],
                                         transpose=True).then_inc(s_xn2.h, 16)
                            s_xn2.n += 16
                        if tc >= 1:
                            emit_stores(tc - 1)
                    emit_stores(NTCB - 1)

                @block.gpsimd
                def _(g):
                    if "bprep" in phases:
                        g.wait_ge(sems["bp_tst"].h, 192 * 16)
                    for tc in range(NTCB):
                        for a in range(64):
                            i = tc * 64 + a
                            if i >= 2:
                                g.wait_ge(s_gmm.h, i - 1)
                            g.dma_start(out=w13sl[:, (a % 2) * 2048:(a % 2 + 1) * 2048]
                                        .rearrange("p (dt h) -> p dt h", h=128),
                                        in_=w1tb.rearrange("(dt p) h -> p dt h", p=128)[
                                            :, :, a * 128:(a + 1) * 128]).then_inc(s_w13.h, 16)
                            g.dma_start(out=w13sl[:, 4096 + (a % 2) * 2048:
                                                  4096 + (a % 2 + 1) * 2048]
                                        .rearrange("p (dt h) -> p dt h", h=128),
                                        in_=w3tb.rearrange("(dt p) h -> p dt h", p=128)[
                                            :, :, a * 128:(a + 1) * 128]).then_inc(s_w13.h, 16)
                            s_w13.n += 32
                        for half in range(2):
                            for gr in range(8):
                                gi = (tc * 2 + half) * 8 + gr
                                if gi >= 2:
                                    g.wait_ge(s_gprog.h, gi - 1)
                                g.dma_start(out=w2sl[:, (gi % 2) * 8192:(gi % 2 + 1) * 8192]
                                            .rearrange("p (ag dd) -> p ag dd", dd=1024),
                                            in_=w2htb.rearrange("(a p) dd -> p a dd", p=128)[
                                                :, gr * 8:(gr + 1) * 8,
                                                half * 1024:(half + 1) * 1024]).then_inc(s_w2ld.h, 16)
                                s_w2ld.n += 16
                        for ts4 in range(4):
                            i = tc * 4 + ts4
                            if i < 2:
                                g.wait_ge(ln3_st.h, TT * 16)
                            else:
                                g.wait_ge(svb.h, (i - 1) * 4)
                            g.dma_start(out=xio[:, (i % 2) * D:(i % 2 + 1) * D],
                                        in_=x2_d[(tc * 4 + ts4) * 128:(tc * 4 + ts4 + 1) * 128, :]
                                        ).then_inc(s_x2ld.h, 16)
                            s_x2ld.n += 16

                @block.tensor
                def _(t):
                    for tc in range(NTCB):
                        t.wait_ge(s_xn2.h, (tc + 1) * 256)
                        for a in range(64):
                            i = tc * 64 + a
                            t.wait_ge(s_w13.h, (i + 1) * 32)
                            pg = PS[2 * (a % 2)]
                            pl = PS[2 * (a % 2) + 1]
                            if a == 0 and tc > 0:
                                t.wait_ge(s_dev.h, tc * 16)
                            if i >= 2:
                                t.wait_ge(s_silu.h, i - 1)
                                t.wait_ge(s_gtt.h, i - 1)
                            for dt in range(16):
                                mm = t.matmul(pg[:, :],
                                              w13sl[:, (a % 2) * 2048 + dt * 128:
                                                    (a % 2) * 2048 + (dt + 1) * 128],
                                              xn2fm[:, dt * 512:(dt + 1) * 512],
                                              start=(dt == 0), stop=(dt == 15))
                            mm.then_inc(s_gate.h, 1)
                            s_gate.n += 1
                            for dt in range(16):
                                mm = t.matmul(pl[:, :],
                                              w13sl[:, 4096 + (a % 2) * 2048 + dt * 128:
                                                    4096 + (a % 2) * 2048 + (dt + 1) * 128],
                                              xn2fm[:, dt * 512:(dt + 1) * 512],
                                              start=(dt == 0), stop=(dt == 15))
                            mm.then_inc(s_gmm.h, 1)
                            s_gmm.n += 1
                            if a > 0:
                                ip = i - 1
                                t.wait_ge(s_gtt.h, ip + 1)
                                py = PS[4 + (ip % 2)]
                                if ip >= 2:
                                    t.wait_ge(s_yev.h, ip - 1)
                                mm = t.matmul(py[:, :], h128_s[:, :],
                                              gbuf[:, (ip % 2) * 512:(ip % 2 + 1) * 512],
                                              start=True, stop=True)
                                mm.then_inc(s_ymm.h, 1)
                                s_ymm.n += 1
                        ip = tc * 64 + 63
                        t.wait_ge(s_gtt.h, ip + 1)
                        py = PS[4 + (ip % 2)]
                        t.wait_ge(s_yev.h, ip - 1)
                        mm = t.matmul(py[:, :], h128_s[:, :],
                                      gbuf[:, (ip % 2) * 512:(ip % 2 + 1) * 512],
                                      start=True, stop=True)
                        mm.then_inc(s_ymm.h, 1)
                        s_ymm.n += 1
                        t.wait_ge(s_yev.h, (tc + 1) * 64)
                        for half in range(2):
                            for gr in range(8):
                                gi = (tc * 2 + half) * 8 + gr
                                t.wait_ge(s_w2ld.h, (gi + 1) * 16)
                                if gr == 0 and (tc > 0 or half > 0):
                                    t.wait_ge(s_dev.h, (tc * 2 + half) * 8)
                                for ag in range(8):
                                    a = gr * 8 + ag
                                    for pair in range(8):
                                        ts4, dc = pair // 2, pair % 2
                                        mm = t.matmul(
                                            PS[pair][:, :],
                                            bufA[:, a * 512 + ts4 * 128: a * 512 + (ts4 + 1) * 128],
                                            w2sl[:, (gi % 2) * 8192 + ag * 1024 + dc * 512:
                                                 (gi % 2) * 8192 + ag * 1024 + (dc + 1) * 512],
                                            start=(a == 0), stop=(a == 63))
                                mm.then_inc(s_gprog.h, 1)
                                s_gprog.n += 1

                @block.scalar
                def _(sc):
                    for tc in range(NTCB):
                        for a in range(64):
                            i = tc * 64 + a
                            sc.wait_ge(s_gate.h, i + 1)
                            pg = PS[2 * (a % 2)]
                            if i >= 2:
                                sc.wait_ge(s_gtt.h, i - 1)
                            sc.activation(silu_s[:, (a % 2) * 512:(a % 2 + 1) * 512],
                                          pg[:, :], AF.Silu).then_inc(s_silu.h, 1)
                            s_silu.n += 1
                        for ts4 in range(4):
                            i = tc * 4 + ts4
                            sc.wait_ge(s_dev.h, (tc + 1) * 16)
                            ln_tile_act(sc, svb, sab, i,
                                        ffn_sb[:, ts4 * 2048:(ts4 + 1) * 2048],
                                        lnb_out[:, 0:D])

                @block.vector
                def _(ve):
                    for tc in range(NTCB):
                        for a in range(64):
                            i = tc * 64 + a
                            ve.wait_ge(s_silu.h, i + 1)
                            ve.wait_ge(s_gmm.h, i + 1)
                            pl = PS[2 * (a % 2) + 1]
                            if i >= 2:
                                ve.wait_ge(s_ymm.h, i - 1)
                            ve.tensor_tensor(gbuf[:, (a % 2) * 512:(a % 2 + 1) * 512],
                                             silu_s[:, (a % 2) * 512:(a % 2 + 1) * 512],
                                             pl[:, :], ALU.mult).then_inc(s_gtt.h, 1)
                            s_gtt.n += 1
                            ve.wait_ge(s_ymm.h, i + 1)
                            py = PS[4 + (a % 2)]
                            ve.tensor_scalar(bufA[:, a * 512:(a + 1) * 512],
                                             py[:, :], RSQRT_N, None,
                                             ALU.mult).then_inc(s_yev.h, 1)
                            s_yev.n += 1
                        for half in range(2):
                            ve.wait_ge(s_gprog.h, (tc * 2 + half) * 8 + 8)
                            for pair in range(8):
                                ts4, dc = pair // 2, pair % 2
                                ins = ve.tensor_copy(
                                    ffn_sb[:, ts4 * 2048 + half * 1024 + dc * 512:
                                           ts4 * 2048 + half * 1024 + (dc + 1) * 512],
                                    PS[pair][:, :])
                                ins.then_inc(s_dev.h, 1)
                                s_dev.n += 1
                        ve.drain()
                        for ts4 in range(4):
                            i = tc * 4 + ts4

                            def extra(ve, i=i, ts4=ts4, tc=tc):
                                ve.wait_ge(s_x2ld.h, (i + 1) * 16)
                                if i >= 1:
                                    ve.wait_ge(s_out.h, i * 16)
                                ve.tensor_tensor(lnb_out[:, D:2 * D],
                                                 lnb_out[:, 0:D],
                                                 xio[:, (i % 2) * D:(i % 2 + 1) * D],
                                                 ALU.add).then_inc(svb.h, 1)
                            ln_tile_vec(ve, svb, sab, i,
                                        ffn_sb[:, ts4 * 2048:(ts4 + 1) * 2048], extra)

                done["b"] = [(s_out, NTCB * 4 * 16)]

        if "b" in phases:
            st_b.close()

        s_fin = sem("final")

        @block.gpsimd
        def _(g):
            for lst in done.values():
                for sm, cn in lst:
                    g.wait_ge(sm.h, cn)
            for n, dap in dbg.items():
                g.dma_start(out=dap, in_=dbg_specs[n][0]).then_inc(s_fin.h, 16)
                s_fin.n += 16
            if s_fin.n:
                g.wait_ge(s_fin.h, s_fin.n)

    return nc


def make_consts():
    def hadamard(n):
        Hm = np.array([[1.0]], dtype=np.float64)
        while Hm.shape[0] < n:
            Hm = np.block([[Hm, Hm], [Hm, -Hm]])
        return Hm
    h128 = hadamard(128).astype(ml_dtypes.bfloat16)
    h64 = hadamard(64).astype(ml_dtypes.bfloat16)
    maskT = np.triu(np.ones((128, 128))).astype(ml_dtypes.bfloat16)
    ident = np.eye(128, dtype=np.float32)
    return h128, h64, maskT, ident


def kernel(L0_attn_hidden_states, L0_attn_Wq, L0_attn_Wk, L0_attn_Wv,
           L0_attn_Wb, L0_attn_Wo, L0_ffn_W1, L0_ffn_W2, L0_ffn_W3):
    hs = np.ascontiguousarray(np.asarray(L0_attn_hidden_states, dtype=np.float32))
    B, S, _ = hs.shape
    T = 2048
    SC = S // NCORES
    h128, h64, maskT, ident = make_consts()
    weights = {
        "wq": np.ascontiguousarray(np.asarray(L0_attn_Wq, dtype=np.float32)),
        "wk": np.ascontiguousarray(np.asarray(L0_attn_Wk, dtype=np.float32)),
        "wv": np.ascontiguousarray(np.asarray(L0_attn_Wv, dtype=np.float32)),
        "wbeta": np.ascontiguousarray(np.asarray(L0_attn_Wb, dtype=np.float32)),
        "wo": np.ascontiguousarray(np.asarray(L0_attn_Wo, dtype=np.float32)),
        "w1": np.ascontiguousarray(np.asarray(L0_ffn_W1, dtype=np.float32)),
        "w2": np.ascontiguousarray(np.asarray(L0_ffn_W2, dtype=np.float32)),
        "w3": np.ascontiguousarray(np.asarray(L0_ffn_W3, dtype=np.float32)),
    }
    nc = build_program(T=T)
    in_maps = []
    for c in range(NCORES):
        pm = np.zeros((128, 8), dtype=np.float32)
        pm[:, :c] = 1.0
        xc = np.ascontiguousarray(hs[:, c * SC:(c + 1) * SC, :].reshape(T, D))
        m = {"x": xc, "h128": h128, "h64": h64, "maskT": maskT,
             "ident": ident, "pmask": pm}
        m.update(weights)
        in_maps.append(m)
    res = run_bass_kernel_spmd(nc, in_maps, core_ids=list(range(NCORES)))
    shards = [res.results[c]["out"].reshape(B, SC, D) for c in range(NCORES)]
    return np.concatenate(shards, axis=1)
